# version 01
# baseline (speedup 1.0000x reference)
import numpy as np
import ml_dtypes
from contextlib import ExitStack

import concourse.bass as bass
import concourse.mybir as mybir
from concourse.bass_utils import run_bass_kernel_spmd

F32 = mybir.dt.float32
BF16 = mybir.dt.bfloat16
AF = mybir.ActivationFunctionType
ALU = mybir.AluOpType
AX = mybir.AxisListType

HID = 2048
OUT_IN = 8512
INTER = 4096
H = 64
P = 64
KCONV = 4
TH = 2051
T = 2048
NCH = 16
CH = 128
NKT = 16
EPS_LN = 1e-5
EPS_RMS = 1e-5
DT_MAX = 100.0
XT0 = 32
BT_ = 64
CT_ = 65
DTT = 66

ENGS = ("pe", "dve", "act", "pool", "sp")


class KB:
    def __init__(self):
        self.ops = {e: [] for e in ENGS}
        self.cnt = {e: 0 for e in ENGS}
        self.last_w = {}
        self.readers = {}
        self.water = {e: {c: 0 for c in ENGS} for e in ENGS}

    def emit(self, eng, fn, reads=(), writes=(), inc=1, name=""):
        waits = {}
        def want(ch, c):
            if ch == "pe" and eng == "pe":
                return
            if c > waits.get(ch, 0):
                waits[ch] = c
        for k in reads:
            lw = self.last_w.get(k)
            if lw is not None:
                want(*lw)
        for k in writes:
            lw = self.last_w.get(k)
            if lw is not None:
                want(*lw)
            for ch, c in self.readers.get(k, {}).items():
                want(ch, c)
        w2 = {ch: c for ch, c in waits.items() if c > self.water[eng][ch]}
        for ch, c in w2.items():
            self.water[eng][ch] = c
        self.cnt[eng] += inc
        after = self.cnt[eng]
        self.ops[eng].append((w2, fn, inc, name))
        for k in writes:
            self.last_w[k] = (eng, after)
            self.readers[k] = {}
        for k in reads:
            rd = self.readers.setdefault(k, {})
            if rd.get(eng, 0) < after:
                rd[eng] = after
        return after

    def group(self, eng, fns, reads=(), writes=(), inc=1, name=""):
        def run(h):
            last = None
            for f in fns:
                last = f(h)
            return last
        return self.emit(eng, run, reads, writes, inc, name)

    def barrier(self):
        snap = dict(self.cnt)
        for e in ENGS:
            w = {}
            for ch, c in snap.items():
                if ch != e and c > self.water[e][ch]:
                    self.water[e][ch] = c
                    w[ch] = c
            if w:
                self.ops[e].append((w, None, 0, "barrier"))

    def finalize(self, block, sems):
        def section(eng):
            def body(h):
                for waits, fn, inc, name in self.ops[eng]:
                    for ch, c in waits.items():
                        h.wait_ge(sems[ch], c)
                    if fn is not None:
                        inst = fn(h)
                        if inc:
                            inst.then_inc(sems[eng], inc)
            return body
        block.tensor(section("pe"))
        block.vector(section("dve"))
        block.scalar(section("act"))
        block.gpsimd(section("pool"))
        block.sync(section("sp"))


def build_program(dev=(), stage=99):
    nc = bass.Bass("TRN2", target_bir_lowering=False)
    DEV = set(dev)
    kb = KB()

    xin = nc.dram_tensor("xin", [TH, HID], F32, kind="ExternalInput").ap()
    win = nc.dram_tensor("win", [HID, OUT_IN], F32, kind="ExternalInput").ap()
    convw = nc.dram_tensor("convw", [KCONV, 4352], F32, kind="ExternalInput").ap()
    convb = nc.dram_tensor("convb", [4352], F32, kind="ExternalInput").ap()
    dtb = nc.dram_tensor("dtb", [H], F32, kind="ExternalInput").ap()
    alog = nc.dram_tensor("alog", [H], F32, kind="ExternalInput").ap()
    dvec = nc.dram_tensor("dvec", [H], F32, kind="ExternalInput").ap()
    nwt = nc.dram_tensor("nwt", [INTER], F32, kind="ExternalInput").ap()
    wout = nc.dram_tensor("wout", [INTER, HID], F32, kind="ExternalInput").ap()
    rwt = nc.dram_tensor("rwt", [8, HID], F32, kind="ExternalInput").ap()
    w1 = nc.dram_tensor("w1", [8, HID, 512], F32, kind="ExternalInput").ap()
    w2 = nc.dram_tensor("w2", [8, 512, HID], F32, kind="ExternalInput").ap()
    w3 = nc.dram_tensor("w3", [8, HID, 512], F32, kind="ExternalInput").ap()
    smask = nc.dram_tensor("smask", [128, 1], F32, kind="ExternalInput").ap()
    out = nc.dram_tensor("out", [T, HID], F32, kind="ExternalOutput").ap()

    sp_xtm = nc.dram_tensor("sp_xtm", [NCH, 128, INTER], BF16).ap()
    sp_z = nc.dram_tensor("sp_z", [32, 128, T], BF16).ap()
    sp_S = nc.dram_tensor("sp_S", [NCH, 128, INTER], BF16).ap()
    sp_yf = nc.dram_tensor("sp_yf", [NCH, 32, 128, CH], BF16).ap()
    sp_h1 = nc.dram_tensor("sp_h1", [NCH, 128, HID], F32).ap()
    sp_part = nc.dram_tensor("sp_part", [NCH, 128, HID], F32).ap()
    sp_g = nc.dram_tensor("sp_g", [4, 4, 128, T], BF16).ap()
    gflat_d = nc.dram_tensor("gflat_d", [NCH, H, CH], F32).ap()
    dce_exp = nc.dram_tensor("dce_exp", [NCH, INTER], F32).ap()
    r_d = nc.dram_tensor("r_d", [T], F32).ap()
    wrow_d = nc.dram_tensor("wrow_d", [8, T], F32).ap()
    cc_in = nc.dram_tensor("cc_in", [128, INTER], F32)
    cc_out = nc.dram_tensor("cc_out", [256, INTER], F32)

    dev_outs = {}
    def devt(name, shape):
        if name in DEV:
            dev_outs[name] = nc.dram_tensor("dev_" + name, shape, F32,
                                            kind="ExternalOutput").ap()
            return dev_outs[name]
        return None

    dv_zx = devt("zx", [67, 128, T])
    dv_conv = devt("conv", [34, 128, T])
    dv_dt = devt("dtp", [4, H, T])
    dv_S = devt("Ssp", [NCH, 128, INTER])
    dv_y = devt("y", [NCH, 128, INTER])
    dv_yf = devt("yf", [NCH, 32, 128, CH])
    dv_rd = devt("rd", [NCH, CH])
    dv_yfsp = devt("yfsp", [NCH, 32, 128, CH])
    dv_wo = devt("wo", [2, 16, 128, HID])
    dv_h1 = devt("h1", [NCH, 128, HID])
    dv_x2 = devt("x2", [NCH, 128, HID])
    dv_sc = devt("scores", [NCH, 128, 8])
    dv_wall = devt("wall", [NCH, 128, 8])

    with ExitStack() as ctx:
        E = ctx.enter_context

        acol = E(nc.sbuf_tensor("acol", [H, 1], F32))
        dtb_sb = E(nc.sbuf_tensor("dtb_sb", [H, 1], F32))
        alog_sb = E(nc.sbuf_tensor("alog_sb", [H, 1], F32))
        convb_sb = E(nc.sbuf_tensor("convb_sb", [128, 34], F32))
        nw_sb = E(nc.sbuf_tensor("nw_sb", [128, 32], F32))
        ident_bf = E(nc.sbuf_tensor("ident_bf", [128, 128], BF16))
        ident_f = E(nc.sbuf_tensor("ident_f", [128, 128], F32))
        triu_f = E(nc.sbuf_tensor("triu_f", [128, 128], F32))
        ones1 = E(nc.sbuf_tensor("ones1", [1, 128], F32))
        ones128_bf = E(nc.sbuf_tensor("ones128_bf", [128, 1], BF16))
        zero_col = E(nc.sbuf_tensor("zero_col", [128, 1], F32))
        smask_sb = E(nc.sbuf_tensor("smask_sb", [128, 1], F32))
        m_c = E(nc.sbuf_tensor("m_c", [128, 1], F32))
        sq_c = E(nc.sbuf_tensor("sq_c", [128, 1], F32))
        v_c = E(nc.sbuf_tensor("v_c", [128, 1], F32))
        r_c = E(nc.sbuf_tensor("r_c", [128, 1], F32))
        nmr_c = E(nc.sbuf_tensor("nmr_c", [128, 1], F32))
        scratch_f = E(nc.sbuf_tensor("scratch_f", [128, HID], F32))

        def dve(fn, reads=(), writes=(), name=""):
            kb.emit("dve", fn, reads, writes, 1, name)

        def act(fn, reads=(), writes=(), name=""):
            kb.emit("act", fn, reads, writes, 1, name)

        def pe1(fn, reads=(), writes=(), name=""):
            kb.emit("pe", fn, reads, writes, 1, name)

        def pool(fn, reads=(), writes=(), name=""):
            kb.emit("pool", fn, reads, writes, 1, name)

        def dma_sp(dst, src, reads=(), writes=(), name=""):
            kb.emit("sp", lambda h: h.dma_start(dst, src), reads, writes, 16, name)

        def dma_pool(dst, src, reads=(), writes=(), name=""):
            kb.emit("pool", lambda h: h.dma_start(dst, src), reads, writes, 16, name)

        def ln_stats_apply(src_sb, src_key, dst_ap, dst_key, rows=128):
            dve(lambda h: h.reduce_sum(m_c[0:rows, :], src_sb[0:rows, :], AX.X),
                (src_key,), ("m_c",))
            act(lambda h: h.activation(scratch_f[0:rows, :], src_sb[0:rows, :],
                                       AF.Square, accum_out=sq_c[0:rows, 0:1]),
                (src_key,), ("sq_c", "scratch_f"))
            dve(lambda h: h.tensor_scalar_mul(m_c[0:rows, :], m_c[0:rows, :], 1.0 / HID),
                ("m_c",), ("m_c",))
            dve(lambda h: h.tensor_scalar_mul(sq_c[0:rows, :], sq_c[0:rows, :], 1.0 / HID),
                ("sq_c",), ("sq_c",))
            dve(lambda h: h.tensor_tensor(v_c[0:rows, :], m_c[0:rows, :], m_c[0:rows, :],
                                          ALU.mult), ("m_c",), ("v_c",))
            dve(lambda h: h.tensor_sub(v_c[0:rows, :], sq_c[0:rows, :], v_c[0:rows, :]),
                ("sq_c", "v_c"), ("v_c",))
            dve(lambda h: h.tensor_scalar_add(v_c[0:rows, :], v_c[0:rows, :], EPS_LN),
                ("v_c",), ("v_c",))
            act(lambda h: h.activation(r_c[0:rows, :], v_c[0:rows, :], AF.Sqrt),
                ("v_c",), ("r_c",))
            dve(lambda h: h.reciprocal(r_c[0:rows, :], r_c[0:rows, :]), ("r_c",), ("r_c",))
            dve(lambda h: h.scalar_tensor_tensor(nmr_c[0:rows, :], m_c[0:rows, :], -1.0,
                                                 r_c[0:rows, :], ALU.mult, ALU.mult),
                ("m_c", "r_c"), ("nmr_c",))
            dve(lambda h: h.tensor_scalar(dst_ap, src_sb[0:rows, :], r_c[0:rows, 0:1],
                                          nmr_c[0:rows, 0:1], ALU.mult, ALU.add),
                (src_key, "r_c", "nmr_c"), (dst_key,))

        colidx = E(nc.sbuf_tensor("colidx", [128, 128], F32))
        partidx = E(nc.sbuf_tensor("partidx", [128, 128], F32))
        pool(lambda h: h.iota(colidx[:], [[1, 128]], channel_multiplier=0,
                              allow_small_or_imprecise_dtypes=True), (), ("colidx",))
        pool(lambda h: h.iota(partidx[:], [[0, 128]], channel_multiplier=1,
                              allow_small_or_imprecise_dtypes=True), (), ("partidx",))
        pool(lambda h: h.memset(ones1[:], 1.0), (), ("ones1",))
        pool(lambda h: h.memset(ones128_bf[:], 1.0), (), ("ones128_bf",))
        pool(lambda h: h.memset(zero_col[:], 0.0), (), ("zero_col",))
        dve(lambda h: h.tensor_tensor(ident_f[:], colidx[:], partidx[:], ALU.is_equal),
            ("colidx", "partidx"), ("ident_f",))
        dve(lambda h: h.tensor_copy(ident_bf[:], ident_f[:]), ("ident_f",), ("ident_bf",))
        dve(lambda h: h.tensor_tensor(triu_f[:], colidx[:], partidx[:], ALU.is_ge),
            ("colidx", "partidx"), ("triu_f",))
        dma_sp(dtb_sb[:, 0:1], bass.AP(dtb.tensor, 0, [[1, H], [1, 1]]), (), ("dtb_sb",))
        dma_sp(alog_sb[:, 0:1], bass.AP(alog.tensor, 0, [[1, H], [1, 1]]), (), ("alog_sb",))
        dma_sp(smask_sb[:], smask[:, :], (), ("smask_sb",))
        kb.emit("sp", lambda h: h.dma_start(convb_sb[:],
                bass.AP(convb.tensor, 0, [[1, 128], [128, 34]]),
                allow_slow_non_contiguous=True), (), ("convb_sb",), 16, "convb")
        kb.emit("sp", lambda h: h.dma_start(nw_sb[:],
                bass.AP(nwt.tensor, 0, [[1, 128], [128, 32]]),
                allow_slow_non_contiguous=True), (), ("nw_sb",), 16, "nw")
        act(lambda h: h.activation(acol[:], alog_sb[:], AF.Exp), ("alog_sb",), ("acol",))
        dve(lambda h: h.tensor_scalar_mul(acol[:], acol[:], -1.0), ("acol",), ("acol",))
        stk_ab = ExitStack()
        ctx.callback(stk_ab.close)
        EA = stk_ab.enter_context
        B_fm = EA(nc.sbuf_tensor("B_fm", [128, T], BF16))
        C_fm = EA(nc.sbuf_tensor("C_fm", [128, T], BF16))
        B_tm = EA(nc.sbuf_tensor("B_tm", [128, T], BF16))
        dt_bf = EA(nc.sbuf_tensor("dt_bf", [H, T], BF16))
        sdec_bf = EA(nc.sbuf_tensor("sdec_bf", [H, T], BF16))
        dtdec_bf = EA(nc.sbuf_tensor("dtdec_bf", [H, T], BF16))
        dsel_bf = EA(nc.sbuf_tensor("dsel_bf", [H, INTER], BF16))
        xtm_buf = [EA(nc.sbuf_tensor(f"xtm{i}", [128, INTER], BF16)) for i in range(2)]

        with ExitStack() as sc1:
            E1 = sc1.enter_context
            xlnT = E1(nc.sbuf_tensor("xlnT", [128, NKT * TH], BF16))
            dt_raw = E1(nc.sbuf_tensor("dt_raw", [H, T], F32))

            with nc.sbuf_tensor("hsel", [H, INTER], F32) as hsel, \
                 nc.sbuf_tensor("hpart", [H, INTER], F32) as hpart:
                pool(lambda h: h.iota(hsel[:], [[1, H], [0, P]], channel_multiplier=0,
                                      allow_small_or_imprecise_dtypes=True), (), ("hsel",))
                pool(lambda h: h.iota(hpart[:], [[0, H], [0, P]], channel_multiplier=1,
                                      allow_small_or_imprecise_dtypes=True), (), ("hpart",))
                dve(lambda h: h.tensor_tensor(hsel[:], hsel[:], hpart[:], ALU.is_equal),
                    ("hsel", "hpart"), ("hsel",))
                dve(lambda h: h.tensor_copy(dsel_bf[:], hsel[:]), ("hsel",), ("dsel_bf",))
                kb.barrier()

            stk_p0 = ExitStack()
            ln_bf = stk_p0.enter_context(nc.sbuf_tensor("ln_bf", [128, HID], BF16))
            xt_ = [stk_p0.enter_context(nc.sbuf_tensor(f"xt{i}", [128, HID], F32))
                   for i in range(2)]
            with nc.psum_tensor("ps_t0", [128, 1024], BF16) as ps_t0, \
                 nc.psum_tensor("ps_t1", [128, 1024], BF16) as ps_t1:
                for tt in range(17):
                    rows = 128 if tt < 16 else 3
                    pb = tt % 2
                    xk = f"xt{pb}"
                    dma_sp(xt_[pb][0:rows, :], xin[tt * 128: tt * 128 + rows, :], (), (xk,),
                           name=f"xld{tt}")
                    ln_stats_apply(xt_[pb], xk, ln_bf[0:rows, :], "ln_bf", rows)
                    for kg in range(2):
                        psq = ps_t0 if kg == 0 else ps_t1
                        pk = f"ps_t{kg}"
                        fns = []
                        for i in range(8):
                            kt = kg * 8 + i
                            def f(h, kt=kt, i=i, psq=psq, rows=rows):
                                return h.transpose(psq[:, i * 128: i * 128 + rows],
                                                   ln_bf[0:rows, kt * 128:(kt + 1) * 128],
                                                   ident_bf[0:rows, 0:rows])
                            fns.append(f)
                        kb.group("pe", fns, ("ln_bf", "ident_bf"), (pk,), name=f"lntr{tt}_{kg}")
                        def ev(h, kg=kg, tt=tt, psq=psq, rows=rows):
                            return h.tensor_copy(
                                bass.AP(xlnT, kg * 8 * TH + tt * 128,
                                        [[NKT * TH, 128], [TH, 8], [1, rows]]),
                                bass.AP(psq, 0, [[1024, 128], [128, 8], [1, rows]]))
                        dve(ev, (pk,), ("xlnT",), name=f"lnev{tt}_{kg}")
            kb.barrier()
            stk_p0.close()

            if stage < 1:
                _fin(nc, kb, ctx)
                return nc, dev_outs

            stk_p1 = ExitStack()
            EP1 = stk_p1.enter_context
            wblk = [EP1(nc.sbuf_tensor(f"wblk{i}", [128, NKT * 128], BF16)) for i in range(2)]
            raw_sb = EP1(nc.sbuf_tensor("raw_sb", [128, TH], BF16))
            xconv_sb = EP1(nc.sbuf_tensor("xconv_sb", [128, T], BF16))
            stage_sb = EP1(nc.sbuf_tensor("stage_sb", [128, T], BF16))
            diag_bf = EP1(nc.sbuf_tensor("diag_bf", [128, KCONV * 128], BF16))
            cwcol = EP1(nc.sbuf_tensor("cwcol", [128, 34 * KCONV], F32))
            for k4_ in range(KCONV):
                kb.emit("sp", lambda h, k4_=k4_: h.dma_start(
                        cwcol[:, k4_ * 34:(k4_ + 1) * 34],
                        bass.AP(convw.tensor, k4_ * 4352, [[1, 128], [128, 34]]),
                        allow_slow_non_contiguous=True), (), ("cwcol",), 16, f"cw{k4_}")

            def load_wblk(ct, buf_i, cw):
                c0 = ct * 128
                dst = bass.AP(wblk[buf_i], 0, [[NKT * 128, 128], [cw, NKT], [1, cw]])
                src = bass.AP(win.tensor, c0, [[OUT_IN, 128], [128 * OUT_IN, NKT], [1, cw]])
                dma_pool(dst, src, (), (f"wblk{buf_i}",), name=f"wld{ct}")

            with nc.psum_tensor("psW", [128, T], F32) as psW, \
                 nc.psum_tensor("psC", [128, 1024], F32) as psC, \
                 nc.psum_tensor("psT", [128, 1024], BF16) as psT:

                def win_matmuls(ct, buf_i, cw):
                    fns = []
                    for k in range(NKT):
                        for nb in range(4):
                            def mm(h, k=k, nb=nb, buf_i=buf_i, cw=cw):
                                return h.matmul(psW[0:cw, nb * 512:(nb + 1) * 512],
                                                wblk[buf_i][:, k * cw:(k + 1) * cw],
                                                xlnT[:, k * TH + nb * 512: k * TH + (nb + 1) * 512],
                                                start=(k == 0), stop=(k == NKT - 1))
                            fns.append(mm)
                        def mmt(h, k=k, buf_i=buf_i, cw=cw):
                            return h.matmul(psC[0:cw, 0:3],
                                            wblk[buf_i][:, k * cw:(k + 1) * cw],
                                            xlnT[:, k * TH + 2048: k * TH + 2051],
                                            start=(k == 0), stop=(k == NKT - 1))
                        fns.append(mmt)
                    kb.group("pe", fns, (f"wblk{buf_i}", "xlnT"), ("psW", "psC"),
                             name=f"winmm{ct}")

                load_wblk(DTT, 0, 64)
                win_matmuls(DTT, 0, 64)
                dve(lambda h: h.tensor_copy(dt_raw[:, 0:2045], psW[0:H, 3:2048]),
                    ("psW",), ("dt_raw",), name="dtev")
                dve(lambda h: h.tensor_copy(dt_raw[:, 2045:2048], psC[0:H, 0:3]),
                    ("psC",), ("dt_raw",), name="dtev3")

                conv_cols = [BT_, CT_] + list(range(XT0, 64))
                load_wblk(conv_cols[0], 1, 128)
                for idx, ct in enumerate(conv_cols):
                    buf_i = (idx + 1) % 2
                    win_matmuls(ct, buf_i, 128)
                    if idx + 1 < len(conv_cols):
                        load_wblk(conv_cols[idx + 1], (idx + 2) % 2, 128)
                    dve(lambda h: h.tensor_copy(raw_sb[:, 0:2048], psW[:, :]),
                        ("psW",), ("raw_sb",), name=f"rawev{ct}")
                    dve(lambda h: h.tensor_copy(raw_sb[:, 2048:2051], psC[:, 0:3]),
                        ("psC",), ("raw_sb",), name=f"rawev3{ct}")
                    if dv_zx is not None:
                        dve(lambda h: h.tensor_copy(scratch_f[:, 0:T], psW[:, :]),
                            ("psW",), ("scratch_f",))
                        dma_sp(dv_zx[ct, :, :], scratch_f[:, 0:T], ("scratch_f",), ())
                    cti = 32 if ct == BT_ else (33 if ct == CT_ else ct - XT0)
                    dst_sb = B_fm if ct == BT_ else (C_fm if ct == CT_ else xconv_sb)
                    dst_key = "B_fm" if ct == BT_ else ("C_fm" if ct == CT_ else "xconv_sb")
                    for k4 in range(KCONV):
                        j = k4 * 34 + cti
                        dve(lambda h, j=j, k4=k4: h.tensor_scalar(
                                diag_bf[:, k4 * 128:(k4 + 1) * 128], ident_f[:],
                                cwcol[:, j:j + 1], None, ALU.mult),
                            ("ident_f", "cwcol"), ("diag_bf",), name=f"diag{ct}_{k4}")
                    for hp2 in range(2):
                        fns = []
                        for k4 in range(KCONV):
                            for nb in range(2):
                                def mm(h, k4=k4, nb=nb, hp2=hp2):
                                    off = k4 + hp2 * 1024 + nb * 512
                                    return h.matmul(psC[:, nb * 512:(nb + 1) * 512],
                                                    diag_bf[:, k4 * 128:(k4 + 1) * 128],
                                                    raw_sb[:, off: off + 512],
                                                    start=(k4 == 0), stop=(k4 == KCONV - 1))
                                fns.append(mm)
                        kb.group("pe", fns, ("diag_bf", "raw_sb"), ("psC",),
                                 name=f"conv{ct}_{hp2}")
                        act(lambda h, dst_sb=dst_sb, cti=cti, hp2=hp2: h.activation(
                                dst_sb[:, hp2 * 1024:(hp2 + 1) * 1024], psC[:, :], AF.Silu,
                                bias=convb_sb[:, cti:cti + 1]),
                            ("psC", "convb_sb"), (dst_key,), name=f"silu{ct}_{hp2}")
                    if dv_conv is not None:
                        dve(lambda h, dst_sb=dst_sb: h.tensor_copy(scratch_f[:, 0:T],
                                                                   dst_sb[:, 0:T]),
                            (dst_key,), ("scratch_f",))
                        dma_sp(dv_conv[cti, :, :], scratch_f[:, 0:T], ("scratch_f",), ())
                    if ct < 64:
                        for cb2 in range(2):
                            fns = []
                            for i in range(8):
                                cb = cb2 * 8 + i
                                def f(h, cb=cb, i=i):
                                    return h.transpose(psT[:, i * 128:(i + 1) * 128],
                                                       xconv_sb[:, cb * 128:(cb + 1) * 128],
                                                       ident_bf[:])
                                fns.append(f)
                            kb.group("pe", fns, ("xconv_sb", "ident_bf"), ("psT",),
                                     name=f"xtr{ct}_{cb2}")
                            dve(lambda h, cb2=cb2: h.tensor_copy(
                                    stage_sb[:, cb2 * 1024:(cb2 + 1) * 1024], psT[:, :]),
                                ("psT",), ("stage_sb",), name=f"xtrev{ct}_{cb2}")
                        ctx_ = ct - XT0
                        dst = bass.AP(sp_xtm.tensor, ctx_ * 128,
                                      [[INTER, 128], [128 * INTER, 16], [1, 128]])
                        src = bass.AP(stage_sb, 0, [[T, 128], [128, 16], [1, 128]])
                        dma_sp(dst, src, ("stage_sb",), ("sp_xtm",), name=f"xtmw{ct}")
                for cb2 in range(2):
                    fns = []
                    for i in range(8):
                        cb = cb2 * 8 + i
                        def f(h, cb=cb, i=i):
                            return h.transpose(psT[:, i * 128:(i + 1) * 128],
                                               B_fm[:, cb * 128:(cb + 1) * 128],
                                               ident_bf[:])
                        fns.append(f)
                    kb.group("pe", fns, ("B_fm", "ident_bf"), ("psT",), name=f"btr{cb2}")
                    dve(lambda h, cb2=cb2: h.tensor_copy(B_tm[:, cb2 * 1024:(cb2 + 1) * 1024],
                                                         psT[:, :]),
                        ("psT",), ("B_tm",), name=f"btrev{cb2}")
            kb.barrier()
            stk_p1.close()

            stk_dt = ExitStack()
            EDT = stk_dt.enter_context
            tA = EDT(nc.sbuf_tensor("tA", [H, T], F32))
            tB = EDT(nc.sbuf_tensor("tB", [H, T], F32))
            tC = EDT(nc.sbuf_tensor("tC", [H, T], F32))
            segmask = EDT(nc.sbuf_tensor("segmask", [H, T], F32))
            dce_sb = EDT(nc.sbuf_tensor("dce_sb", [H, NCH], F32))
            pool(lambda h: h.iota(tA[:], [[0, NCH], [1, CH]], channel_multiplier=0,
                                  allow_small_or_imprecise_dtypes=True), (), ("tA",))
            dve(lambda h: h.tensor_scalar(segmask[:], tA[:], 0.5, None, ALU.is_ge),
                ("tA",), ("segmask",))
            act(lambda h: h.activation(tA[:], dt_raw[:], AF.Abs, bias=dtb_sb[:, 0:1]),
                ("dt_raw", "dtb_sb"), ("tA",))
            act(lambda h: h.activation(tB[:], tA[:], AF.Exp, scale=-1.0), ("tA",), ("tB",))
            act(lambda h: h.activation(tA[:], tB[:], AF.Ln, bias=1.0), ("tB",), ("tA",))
            act(lambda h: h.activation(tB[:], dt_raw[:], AF.Relu, bias=dtb_sb[:, 0:1]),
                ("dt_raw", "dtb_sb"), ("tB",))
            dve(lambda h: h.tensor_add(tC[:], tA[:], tB[:]), ("tA", "tB"), ("tC",))
            dve(lambda h: h.tensor_scalar(tC[:], tC[:], DT_MAX, None, ALU.min),
                ("tC",), ("tC",))
            dve(lambda h: h.tensor_copy(dt_bf[:], tC[:]), ("tC",), ("dt_bf",))
            if dv_dt is not None:
                dma_sp(dv_dt[0, :, :], tC[:], ("tC",), ())
            dve(lambda h: h.tensor_scalar(tA[:], tC[:], acol[:, 0:1], None, ALU.mult),
                ("tC", "acol"), ("tA",), name="dA")
            dve(lambda h: h.tensor_tensor_scan(tB[:], segmask[:], tA[:], zero_col[0:H, 0:1],
                                               ALU.mult, ALU.add),
                ("segmask", "tA", "zero_col"), ("tB",), name="Acs")
            if dv_dt is not None:
                dma_sp(dv_dt[1, :, :], tB[:], ("tB",), ())
            act(lambda h: h.activation(tA[:], tA[:], AF.Exp), ("tA",), ("tA",), name="gexp")
            dve(lambda h: h.tensor_tensor(tA[:], tA[:], segmask[:], ALU.mult),
                ("tA", "segmask"), ("tA",), name="gmask")
            dma_sp(bass.AP(gflat_d.tensor, 0, [[CH, H], [H * CH, NCH], [1, CH]]), tA[:],
                   ("tA",), ("gflat_d",), name="gfl")
            act(lambda h: h.activation(sdec_bf[:], tB[:], AF.Exp), ("tB",), ("sdec_bf",))
            act(lambda h: h.activation(dce_sb[:], bass.AP(tB, 127, [[T, H], [CH, NCH]]),
                                       AF.Exp), ("tB",), ("dce_sb",))
            with nc.sbuf_tensor("dsf", [H, INTER], F32) as dsf, \
                 nc.sbuf_tensor("dsp", [H, INTER], F32) as dsp, \
                 nc.psum_tensor("ps_dce", [16, INTER], F32) as ps_dce:
                pool(lambda h: h.iota(dsf[:], [[1, H], [0, P]], channel_multiplier=0,
                                      allow_small_or_imprecise_dtypes=True), (), ("dsf",))
                pool(lambda h: h.iota(dsp[:], [[0, H], [0, P]], channel_multiplier=1,
                                      allow_small_or_imprecise_dtypes=True), (), ("dsp",))
                dve(lambda h: h.tensor_tensor(dsf[:], dsf[:], dsp[:], ALU.is_equal),
                    ("dsf", "dsp"), ("dsf",))
                fns = []
                for i in range(8):
                    def mm(h, i=i):
                        return h.matmul(ps_dce[:, i * 512:(i + 1) * 512],
                                        dce_sb[:], dsf[:, i * 512:(i + 1) * 512],
                                        start=True, stop=True)
                    fns.append(mm)
                kb.group("pe", fns, ("dce_sb", "dsf"), ("ps_dce",), name="dcemm")
                dve(lambda h: h.tensor_copy(dsp[0:16, :], ps_dce[:, :]),
                    ("ps_dce",), ("dsp",), name="dceev")
                dma_sp(dce_exp[:, :], dsp[0:16, :], ("dsp",), ("dce_exp",), name="dcew")
                kb.barrier()
            for c in range(NCH):
                act(lambda h, c=c: h.activation(tA[:, c * CH:(c + 1) * CH],
                                                tB[:, c * CH:(c + 1) * CH], AF.Exp,
                                                bias=tB[:, c * CH + 127: c * CH + 128],
                                                scale=-1.0),
                    ("tB",), ("tA",), name=f"decay{c}")
            dve(lambda h: h.tensor_tensor(dtdec_bf[:], tC[:], tA[:], ALU.mult),
                ("tC", "tA"), ("dtdec_bf",))
            if dv_dt is not None:
                dve(lambda h: h.tensor_copy(tB[:], dtdec_bf[:]), ("dtdec_bf",), ("tB",))
                dma_sp(dv_dt[2, :, :], tB[:], ("tB",), ())
                dve(lambda h: h.tensor_copy(tB[:], sdec_bf[:]), ("sdec_bf",), ("tB",))
                dma_sp(dv_dt[3, :, :], tB[:], ("tB",), ())
            kb.barrier()
            stk_dt.close()
            if stage < 2:
                _fin(nc, kb, ctx)
                return nc, dev_outs

            stk_s1 = ExitStack()
            ES1 = stk_s1.enter_context
            S_sb = ES1(nc.sbuf_tensor("S_sb", [128, INTER], F32))
            S_bf = ES1(nc.sbuf_tensor("S_bf", [128, INTER], BF16))
            xdtd = ES1(nc.sbuf_tensor("xdtd", [128, INTER], BF16))
            tS = ES1(nc.sbuf_tensor("tS", [128, INTER], F32))
            dce_row = ES1(nc.sbuf_tensor("dce_row", [1, INTER], F32))
            pool(lambda h: h.memset(S_sb[:], 0.0), (), ("S_sb",))
            with nc.psum_tensor("psA", [128, 2048], F32) as psA, \
                 nc.psum_tensor("psB", [128, 1024], F32) as psB, \
                 nc.psum_tensor("psD", [128, 1024], F32) as psD:
                dma_sp(xtm_buf[0][:], sp_xtm[0, :, :], ("sp_xtm",), ("xtm0",), name="p1x0")
                for c in range(NCH):
                    bi = c % 2
                    xk = f"xtm{bi}"
                    if c + 1 < NCH:
                        dma_sp(xtm_buf[(c + 1) % 2][:], sp_xtm[c + 1, :, :],
                               ("sp_xtm",), (f"xtm{(c + 1) % 2}",), name=f"p1x{c + 1}")
                    dve(lambda h: h.tensor_copy(S_bf[:], S_sb[:]), ("S_sb",), ("S_bf",))
                    dma_sp(sp_S[c, :, :], S_bf[:], ("S_bf",), (f"sp_S{c}",), name=f"p1Sw{c}")
                    if dv_S is not None:
                        dve(lambda h: h.tensor_copy(tS[:], S_sb[:]), ("S_sb",), ("tS",))
                        dma_sp(dv_S[c, :, :], tS[:], ("tS",), ())
                    dma_sp(dce_row[:], dce_exp[c:c + 1, :],
                           ("dce_exp",), ("dce_row",), name=f"dcer{c}")
                    for hf in range(2):
                        fns = []
                        for i in range(4):
                            def mm(h, hf=hf, i=i, c=c):
                                return h.matmul(psA[:, i * 512:(i + 1) * 512],
                                                dtdec_bf[:, c * CH:(c + 1) * CH],
                                                dsel_bf[:, hf * 2048 + i * 512: hf * 2048 + (i + 1) * 512],
                                                start=True, stop=True)
                            fns.append(mm)
                        kb.group("pe", fns, ("dtdec_bf", "dsel_bf"), ("psA",),
                                 name=f"p1bc{c}_{hf}")
                        dve(lambda h, hf=hf, bi=bi: h.tensor_tensor(
                                xdtd[:, hf * 2048:(hf + 1) * 2048],
                                xtm_buf[bi][:, hf * 2048:(hf + 1) * 2048],
                                psA[:, :], ALU.mult),
                            (xk, "psA"), ("xdtd",), name=f"p1xd{c}_{hf}")
                        for sh in range(2):
                            q0 = hf * 2048 + sh * 1024
                            fns = []
                            for i in range(2):
                                def mm(h, q0=q0, i=i, c=c):
                                    return h.matmul(psB[:, i * 512:(i + 1) * 512],
                                                    B_tm[:, c * CH:(c + 1) * CH],
                                                    xdtd[:, q0 + i * 512: q0 + (i + 1) * 512],
                                                    start=True, stop=True)
                                fns.append(mm)
                            kb.group("pe", fns, ("B_tm", "xdtd"), ("psB",),
                                     name=f"p1st{c}_{hf}_{sh}")
                            fns2 = []
                            for i in range(2):
                                def mm2(h, q0=q0, i=i):
                                    return h.matmul(psD[:, i * 512:(i + 1) * 512],
                                                    ones1[:, :],
                                                    dce_row[:, q0 + i * 512: q0 + (i + 1) * 512],
                                                    start=True, stop=True)
                                fns2.append(mm2)
                            kb.group("pe", fns2, ("dce_row", "ones1"), ("psD",),
                                     name=f"p1dc{c}_{hf}_{sh}")
                            dve(lambda h, q0=q0: h.tensor_tensor(tS[:, q0:q0 + 1024],
                                                                 S_sb[:, q0:q0 + 1024],
                                                                 psD[:, :], ALU.mult),
                                ("S_sb", "psD"), ("tS",), name=f"p1u1_{c}_{q0}")
                            dve(lambda h, q0=q0: h.tensor_tensor(S_sb[:, q0:q0 + 1024],
                                                                 tS[:, q0:q0 + 1024],
                                                                 psB[:, :], ALU.add),
                                ("tS", "psB"), ("S_sb",), name=f"p1u2_{c}_{q0}")
            kb.barrier()

            dve(lambda h: h.tensor_copy(tS[:], S_sb[:]), ("S_sb",), ("tS",))
            dma_sp(cc_in.ap()[:, :], tS[:], ("tS",), ("cc_in",))
            kb.emit("pool", lambda h: h.collective_compute(
                        "AllGather", ALU.bypass, [[0, 1], [2, 3], [4, 5], [6, 7]],
                        [cc_in.ap().opt()], [cc_out.ap().opt()]),
                    ("cc_in",), ("cc_out",), 1, name="ccag")
            kb.barrier()
            stk_s1.close()
            if stage < 3:
                _fin(nc, kb, ctx)
                return nc, dev_outs

            stk_p2 = ExitStack()
            EP2 = stk_p2.enter_context
            wblk = [EP2(nc.sbuf_tensor(f"wblkz{i}", [128, NKT * 128], BF16)) for i in range(2)]
            raw_sb = EP2(nc.sbuf_tensor("raw_sbz", [128, TH], BF16))

            def load_wblk2(ct, buf_i, cw):
                c0 = ct * 128
                dst = bass.AP(wblk[buf_i], 0, [[NKT * 128, 128], [cw, NKT], [1, cw]])
                src = bass.AP(win.tensor, c0, [[OUT_IN, 128], [128 * OUT_IN, NKT], [1, cw]])
                dma_pool(dst, src, (), (f"wblk{buf_i}",), name=f"wld{ct}")

            with nc.psum_tensor("psW2", [128, T], F32) as psW2:
                load_wblk2(0, 0, 128)
                for ct in range(32):
                    buf_i = ct % 2
                    if ct + 1 < 32:
                        load_wblk2(ct + 1, (ct + 1) % 2, 128)
                    fns = []
                    for k in range(NKT):
                        for nb in range(4):
                            def mm(h, k=k, nb=nb, buf_i=buf_i):
                                return h.matmul(psW2[:, nb * 512:(nb + 1) * 512],
                                                wblk[buf_i][:, k * 128:(k + 1) * 128],
                                                xlnT[:, k * TH + 3 + nb * 512: k * TH + 3 + (nb + 1) * 512],
                                                start=(k == 0), stop=(k == NKT - 1))
                            fns.append(mm)
                    kb.group("pe", fns, (f"wblk{buf_i}", "xlnT"), ("psW2",), name=f"zmm{ct}")
                    dve(lambda h: h.tensor_copy(raw_sb[:, 0:T], psW2[:, :]),
                        ("psW2",), ("raw_sb",), name=f"zev{ct}")
                    dma_sp(sp_z[ct, :, :], raw_sb[:, 0:T], ("raw_sb",), ("sp_z",),
                           name=f"zw{ct}")
                    if dv_zx is not None:
                        dve(lambda h: h.tensor_copy(scratch_f[:, 0:T], psW2[:, :]),
                            ("psW2",), ("scratch_f",))
                        dma_sp(dv_zx[ct, :, :], scratch_f[:, 0:T], ("scratch_f",), ())
            kb.barrier()
            stk_p2.close()

        if stage < 4:
            _fin(nc, kb, ctx)
            return nc, dev_outs

        with ExitStack() as sc2:
            E2 = sc2.enter_context
            z_fm = [E2(nc.sbuf_tensor(f"zfm{i}", [128, INTER], BF16)) for i in range(2)]
            Ssp_sb = E2(nc.sbuf_tensor("Ssp_sb", [128, INTER], BF16))
            zs_bf = E2(nc.sbuf_tensor("zs_bf", [128, INTER], BF16))
            y_sb = E2(nc.sbuf_tensor("y_sb", [128, INTER], F32))
            yf_bf = E2(nc.sbuf_tensor("yf_bf", [128, INTER], BF16))
            L_sb = E2(nc.sbuf_tensor("L_sb", [128, 2048], F32))
            M_bf = E2(nc.sbuf_tensor("M_bf", [128, 2048], BF16))
            xdt2 = E2(nc.sbuf_tensor("xdt2", [128, INTER], BF16))
            xD2 = E2(nc.sbuf_tensor("xD2", [128, INTER], BF16))
            sd_sb = E2(nc.sbuf_tensor("sd_sb", [128, 1024], BF16))
            scm_bf = E2(nc.sbuf_tensor("scm_bf", [128, 128], BF16))
            rowbuf = E2(nc.sbuf_tensor("rowbuf", [1, INTER], F32))
            r_row = E2(nc.sbuf_tensor("r_row", [1, 128], F32))
            Scar = E2(nc.sbuf_tensor("Scar", [128, INTER], F32))
            Scar_bf = E2(nc.sbuf_tensor("Scar_bf", [128, INTER], BF16))
            Dbx_bf = E2(nc.sbuf_tensor("Dbx_bf", [128, INTER], BF16))
            identrep = E2(nc.sbuf_tensor("identrep", [128, 2048], F32))

            with nc.psum_tensor("ps_a", [128, 2048], F32) as ps_a, \
                 nc.psum_tensor("ps_y", [128, 1024], F32) as ps_y, \
                 nc.psum_tensor("ps_yo", [128, 1024], F32) as ps_yo:

                dve(lambda h: h.tensor_copy(identrep[:],
                                            bass.AP(ident_f, 0, [[128, 128], [0, 16], [1, 128]])),
                    ("ident_f",), ("identrep",), name="identrep")
                with nc.sbuf_tensor("drep", [H, 128], BF16) as drep, \
                     nc.sbuf_tensor("dcol", [H, 1], F32) as dcol:
                    dma_sp(dcol[:, 0:1], bass.AP(dvec.tensor, 0, [[1, H], [1, 1]]),
                           (), ("dcol",), name="dcol")
                    pool(lambda h: h.memset(drep[:], 1.0), (), ("drep",))
                    dve(lambda h: h.tensor_scalar(drep[:], drep[:], dcol[:, 0:1], None,
                                                  ALU.mult),
                        ("drep", "dcol"), ("drep",), name="drepsc")
                    for hf in range(2):
                        fns = []
                        for i in range(4):
                            def mm(h, hf=hf, i=i):
                                return h.matmul(ps_a[:, i * 512:(i + 1) * 512], drep[:],
                                                dsel_bf[:, hf * 2048 + i * 512: hf * 2048 + (i + 1) * 512],
                                                start=True, stop=True)
                            fns.append(mm)
                        kb.group("pe", fns, ("drep", "dsel_bf"), ("ps_a",), name=f"dbx{hf}")
                        dve(lambda h, hf=hf: h.tensor_copy(
                                Dbx_bf[:, hf * 2048:(hf + 1) * 2048], ps_a[:, :]),
                            ("ps_a",), ("Dbx_bf",), name=f"dbxev{hf}")
                    kb.barrier()
                dma_sp(y_sb[:], cc_out.ap()[0:128, :], ("cc_out",), ("y_sb",), name="ccr")
                dve(lambda h: h.tensor_scalar(Scar[:], y_sb[:], smask_sb[:, 0:1], None,
                                              ALU.mult),
                    ("y_sb", "smask_sb"), ("Scar",), name="scar_init")
                dve(lambda h: h.tensor_copy(Scar_bf[:], Scar[:]), ("Scar",), ("Scar_bf",))

                dma_sp(xtm_buf[0][:], sp_xtm[0, :, :], ("sp_xtm",), ("xtm0",), name="p2x0")
                dma_sp(bass.AP(z_fm[0], 0, [[INTER, 128], [128, 32], [1, 128]]),
                       bass.AP(sp_z.tensor, 0, [[T, 128], [128 * T, 32], [1, 128]]),
                       ("sp_z",), ("zfm0",), name="p2z0")

                for c in range(NCH):
                    bi = c % 2
                    xk, zk, sk = f"xtm{bi}", f"zfm{bi}", "Ssp"
                    dma_sp(Ssp_sb[:], sp_S[c, :, :], (f"sp_S{c}",), ("Ssp",),
                           name=f"p2S{c}")
                    if c + 1 < NCH:
                        nb_ = (c + 1) % 2
                        dma_sp(xtm_buf[nb_][:], sp_xtm[c + 1, :, :],
                               ("sp_xtm",), (f"xtm{nb_}",), name=f"p2x{c + 1}")
                        dma_sp(bass.AP(z_fm[nb_], 0, [[INTER, 128], [128, 32], [1, 128]]),
                               bass.AP(sp_z.tensor, (c + 1) * CH,
                                       [[T, 128], [128 * T, 32], [1, 128]]),
                               ("sp_z",), (f"zfm{nb_}",), name=f"p2z{c + 1}")
                    for hf in range(2):
                        fns = []
                        for i in range(4):
                            def mm(h, hf=hf, i=i, c=c):
                                return h.matmul(ps_a[:, i * 512:(i + 1) * 512],
                                                dt_bf[:, c * CH:(c + 1) * CH],
                                                dsel_bf[:, hf * 2048 + i * 512: hf * 2048 + (i + 1) * 512],
                                                start=True, stop=True)
                            fns.append(mm)
                        kb.group("pe", fns, ("dt_bf", "dsel_bf"), ("ps_a",),
                                 name=f"dtbc{c}_{hf}")
                        dve(lambda h, hf=hf, bi=bi: h.tensor_tensor(
                                xdt2[:, hf * 2048:(hf + 1) * 2048],
                                xtm_buf[bi][:, hf * 2048:(hf + 1) * 2048],
                                ps_a[:, :], ALU.mult),
                            (xk, "ps_a"), ("xdt2",), name=f"xdt{c}_{hf}")
                    dve(lambda h, bi=bi: h.tensor_tensor(xD2[:], xtm_buf[bi][:],
                                                         Dbx_bf[:], ALU.mult),
                        (xk, "Dbx_bf"), ("xD2",), name=f"xD{c}")
                    pe1(lambda h, c=c: h.matmul(ps_a[:, 0:128], B_fm[:, c * CH:(c + 1) * CH],
                                                C_fm[:, c * CH:(c + 1) * CH],
                                                start=True, stop=True),
                        ("B_fm", "C_fm"), ("ps_a",), name=f"sc{c}")
                    dve(lambda h: h.tensor_tensor(scm_bf[:], ps_a[:, 0:128], triu_f[:],
                                                  ALU.mult),
                        ("ps_a", "triu_f"), ("scm_bf",), name=f"scm{c}")
                    act(lambda h, bi=bi: h.activation(zs_bf[:], z_fm[bi][:], AF.Silu),
                        (zk,), ("zs_bf",), name=f"siluz{c}")
                    for g4 in range(4):
                        if g4 % 2 == 0:
                            dma_sp(rowbuf[:], bass.AP(gflat_d.tensor,
                                                      c * H * CH + (g4 // 2) * 4096,
                                                      [[1, 1], [1, 4096]]),
                                   ("gflat_d",), ("rowbuf",), name=f"p2gf{c}_{g4}")
                        go = (g4 % 2) * 2048
                        fns = []
                        for i in range(4):
                            def mm(h, go=go, i=i):
                                return h.matmul(ps_a[:, i * 512:(i + 1) * 512],
                                                ones1[:, :],
                                                rowbuf[:, go + i * 512: go + (i + 1) * 512],
                                                start=True, stop=True)
                            fns.append(mm)
                        kb.group("pe", fns, ("rowbuf", "ones1"), ("ps_a",),
                                 name=f"gbc{c}_{g4}")
                        dve(lambda h: h.tensor_tensor_scan(
                                L_sb[:], ps_a[:, :], identrep[:],
                                zero_col[:, 0:1], ALU.mult, ALU.add),
                            ("ps_a", "identrep", "zero_col"), ("L_sb",),
                            name=f"Lscan{c}_{g4}")
                        dve(lambda h: h.tensor_tensor(
                                M_bf[:], L_sb[:],
                                bass.AP(scm_bf, 0, [[128, 128], [0, 16], [1, 128]]),
                                ALU.mult),
                            ("L_sb", "scm_bf"), ("M_bf",), name=f"Mc{c}_{g4}")
                        fns = []
                        for h16 in range(16):
                            def mm(h, h16=h16, g4=g4):
                                hh = g4 * 16 + h16
                                return h.matmul(ps_y[:, h16 * 64:(h16 + 1) * 64],
                                                M_bf[:, h16 * 128:(h16 + 1) * 128],
                                                xdt2[:, hh * 64:(hh + 1) * 64],
                                                start=(h16 % 8 == 0), stop=False)
                            fns.append(mm)
                        for i in range(2):
                            def mm(h, i=i, g4=g4):
                                return h.matmul(ps_y[:, i * 512:(i + 1) * 512],
                                                ident_bf[:],
                                                xD2[:, g4 * 1024 + i * 512: g4 * 1024 + (i + 1) * 512],
                                                start=False, stop=True)
                            fns.append(mm)
                        kb.group("pe", fns, ("M_bf", "xdt2", "ident_bf", "xD2"), ("ps_y",),
                                 name=f"ydiag{c}_{g4}")
                        fns = []
                        for i in range(2):
                            def mm(h, i=i, g4=g4, c=c):
                                return h.matmul(ps_yo[:, i * 512:(i + 1) * 512],
                                                C_fm[:, c * CH:(c + 1) * CH],
                                                Ssp_sb[:, g4 * 1024 + i * 512: g4 * 1024 + (i + 1) * 512],
                                                start=True, stop=False)
                            fns.append(mm)
                        for i in range(2):
                            def mm(h, i=i, g4=g4, c=c):
                                return h.matmul(ps_yo[:, i * 512:(i + 1) * 512],
                                                C_fm[:, c * CH:(c + 1) * CH],
                                                Scar_bf[:, g4 * 1024 + i * 512: g4 * 1024 + (i + 1) * 512],
                                                start=False, stop=True)
                            fns.append(mm)
                        kb.group("pe", fns, ("C_fm", sk, "Scar_bf"), ("ps_yo",),
                                 name=f"yoff{c}_{g4}")
                        fns = []
                        for i in range(2):
                            def mm(h, g4=g4, i=i, c=c):
                                return h.matmul(ps_a[:, i * 512:(i + 1) * 512],
                                                sdec_bf[:, c * CH:(c + 1) * CH],
                                                dsel_bf[:, g4 * 1024 + i * 512: g4 * 1024 + (i + 1) * 512],
                                                start=True, stop=True)
                            fns.append(mm)
                        kb.group("pe", fns, ("sdec_bf", "dsel_bf"), ("ps_a",),
                                 name=f"sdbc{c}_{g4}")
                        dve(lambda h: h.tensor_copy(sd_sb[:], ps_a[:, 0:1024]),
                            ("ps_a",), ("sd_sb",), name=f"sdev{c}_{g4}")
                        dve(lambda h: h.tensor_tensor(L_sb[:, 0:1024], sd_sb[:],
                                                      ps_yo[:, :], ALU.mult),
                            ("sd_sb", "ps_yo"), ("L_sb",), name=f"yo1{c}_{g4}")
                        dve(lambda h, g4=g4: h.tensor_tensor(
                                y_sb[:, g4 * 1024:(g4 + 1) * 1024],
                                L_sb[:, 0:1024], ps_y[:, :], ALU.add),
                            ("L_sb", "ps_y"), ("y_sb",), name=f"yo2{c}_{g4}")
                    if dv_y is not None:
                        dma_sp(dv_y[c, :, :], y_sb[:], ("y_sb",), ())
                    for q in range(4):
                        fns = []
                        for i in range(8):
                            kt = q * 8 + i
                            def f(h, kt=kt, i=i):
                                return h.transpose(ps_y[:, i * 128:(i + 1) * 128],
                                                   y_sb[:, kt * 128:(kt + 1) * 128],
                                                   ident_f[:])
                            fns.append(f)
                        kb.group("pe", fns, ("y_sb", "ident_f"), ("ps_y",),
                                 name=f"ytr{c}_{q}")
                        dve(lambda h, q=q: h.tensor_tensor(yf_bf[:, q * 1024:(q + 1) * 1024],
                                                           ps_y[:, :],
                                                           zs_bf[:, q * 1024:(q + 1) * 1024],
                                                           ALU.mult),
                            ("ps_y", "zs_bf"), ("yf_bf",), name=f"gate{c}_{q}")
                    dve(lambda h: h.tensor_tensor(xdt2[:], yf_bf[:], yf_bf[:], ALU.mult),
                        ("yf_bf",), ("xdt2",), name=f"sq{c}")
                    fns = []
                    for kt in range(32):
                        def mm(h, kt=kt):
                            return h.matmul(ps_yo[0:1, 0:128], ones128_bf[:, 0:1],
                                            xdt2[:, kt * 128:(kt + 1) * 128],
                                            start=(kt == 0), stop=(kt == 31))
                        fns.append(mm)
                    kb.group("pe", fns, ("xdt2", "ones128_bf"), ("ps_yo",),
                             name=f"var{c}")
                    dve(lambda h: h.tensor_scalar(r_row[:], ps_yo[0:1, 0:128], 1.0 / INTER,
                                                  EPS_RMS, ALU.mult, ALU.add),
                        ("ps_yo",), ("r_row",), name=f"v1{c}")
                    act(lambda h: h.activation(r_row[:], r_row[:], AF.Sqrt),
                        ("r_row",), ("r_row",), name=f"v2{c}")
                    dve(lambda h: h.reciprocal(r_row[:], r_row[:]), ("r_row",), ("r_row",),
                        name=f"v3{c}")
                    dma_sp(bass.AP(r_d.tensor, c * CH, [[1, 1], [1, CH]]), r_row[:],
                           ("r_row",), ("r_d",), name=f"rw{c}")
                    if dv_rd is not None:
                        dma_sp(dv_rd[c:c + 1, :], r_row[:], ("r_row",), ())
                    if dv_yf is not None:
                        dma_pool(bass.AP(dv_yf.tensor, c * 32 * 128 * CH,
                                         [[CH, 128], [128 * CH, 32], [1, CH]]),
                                 bass.AP(yf_bf, 0, [[INTER, 128], [CH, 32], [1, CH]]),
                                 ("yf_bf",), ())
                    dma_sp(bass.AP(sp_yf.tensor, c * 32 * 128 * CH,
                                   [[CH, 128], [128 * CH, 32], [1, CH]]),
                           bass.AP(yf_bf, 0, [[INTER, 128], [CH, 32], [1, CH]]),
                           ("yf_bf",), (f"sp_yf{c}",), name=f"yfw{c}")
                    if c + 1 < NCH:
                        dma_sp(rowbuf[:], dce_exp[c:c + 1, :],
                               ("dce_exp",), ("rowbuf",), name=f"p2dcer{c}")
                        for hf in range(2):
                            fns = []
                            for i in range(4):
                                def mm(h, hf=hf, i=i):
                                    return h.matmul(ps_a[:, i * 512:(i + 1) * 512],
                                                    ones1[:, :],
                                                    rowbuf[:, hf * 2048 + i * 512: hf * 2048 + (i + 1) * 512],
                                                    start=True, stop=True)
                                fns.append(mm)
                            kb.group("pe", fns, ("rowbuf", "ones1"), ("ps_a",),
                                     name=f"p2dc{c}_{hf}")
                            dve(lambda h, hf=hf: h.tensor_tensor(
                                    y_sb[:, hf * 2048: hf * 2048 + 2048],
                                    Scar[:, hf * 2048: hf * 2048 + 2048],
                                    ps_a[:, :], ALU.mult),
                                ("Scar", "ps_a"), ("y_sb",),
                                name=f"scar{c}_{hf}")
                        dve(lambda h: h.tensor_copy(Scar[:], y_sb[:]),
                            ("y_sb",), ("Scar",), name=f"scarcp{c}")
                        dve(lambda h: h.tensor_copy(Scar_bf[:], Scar[:]),
                            ("Scar",), ("Scar_bf",), name=f"scarbf{c}")
            kb.barrier()

        stk_ab.close()
        if stage < 5:
            _fin(nc, kb, ctx)
            return nc, dev_outs

        x2_fm = E(nc.sbuf_tensor("x2_fm", [128, NKT * T], BF16))
        with ExitStack() as sc3:
            E3 = sc3.enter_context
            x2t = E3(nc.sbuf_tensor("x2t", [128, HID], BF16))
            wo_sb = E3(nc.sbuf_tensor("wo_sb", [128, 16 * HID], BF16))
            yft = [E3(nc.sbuf_tensor(f"yft{i}", [128, 16 * 128], BF16)) for i in range(2)]
            xres = [E3(nc.sbuf_tensor(f"xres{i}", [128, HID], F32)) for i in range(2)]
            part_b = [E3(nc.sbuf_tensor(f"part{i}", [128, HID], F32)) for i in range(2)]
            h1_sb = E3(nc.sbuf_tensor("h1_sb", [128, HID], F32))
            rcol = E3(nc.sbuf_tensor("rcol", [128, 1], F32))

            with nc.psum_tensor("ps_w", [128, HID], F32) as ps_w, \
                 nc.psum_tensor("ps_tx", [128, 1024], BF16) as ps_tx:
                for half in range(2):
                    for kt in range(16):
                        gk = half * 16 + kt
                        dma_sp(scratch_f[:], wout[gk * 128:(gk + 1) * 128, :],
                               (), ("scratch_f",), name=f"wold{gk}")
                        dve(lambda h, kt=kt, gk=gk: h.tensor_scalar(
                                wo_sb[:, kt * HID:(kt + 1) * HID],
                                scratch_f[:],
                                nw_sb[:, gk:gk + 1], None, ALU.mult),
                            ("scratch_f", "nw_sb"), (f"wo{kt}",), name=f"wosc{gk}")
                    if dv_wo is not None:
                        for kt in range(16):
                            dve(lambda h, kt=kt: h.tensor_copy(scratch_f[:],
                                    wo_sb[:, kt * HID:(kt + 1) * HID]),
                                (f"wo{kt}",), ("scratch_f",))
                            dma_sp(dv_wo[half, kt, :, :], scratch_f[:], ("scratch_f",), ())
                    if dv_yfsp is not None and half == 0:
                        for c_ in range(NCH):
                            kb.emit("pool", lambda h, c_=c_: h.dma_start(
                                    dv_yfsp[c_, :, :, :], sp_yf[c_, :, :, :]),
                                    (f"sp_yf{c_}",), (), 16, f"yfspd{c_}")
                    for tt in range(NCH):
                        pb = tt % 2
                        dma_sp(bass.AP(yft[pb], 0, [[16 * 128, 128], [128, 16], [1, 128]]),
                               bass.AP(sp_yf.tensor,
                                       tt * 32 * 128 * CH + half * 16 * 128 * CH,
                                       [[CH, 128], [128 * CH, 16], [1, CH]]),
                               (f"sp_yf{tt}",), (f"yft{pb}",), name=f"yfr{half}_{tt}")
                        fns = []
                        for kt in range(16):
                            for nb in range(4):
                                def mm(h, kt=kt, nb=nb, pb=pb):
                                    return h.matmul(ps_w[:, nb * 512:(nb + 1) * 512],
                                                    yft[pb][:, kt * 128:(kt + 1) * 128],
                                                    wo_sb[:, kt * HID + nb * 512: kt * HID + (nb + 1) * 512],
                                                    start=(kt == 0), stop=(kt == 15))
                                fns.append(mm)
                        kb.group("pe", fns,
                                 tuple([f"yft{pb}"] + [f"wo{k}" for k in range(16)]),
                                 ("ps_w",), name=f"womm{half}_{tt}")
                        if half == 0:
                            dve(lambda h, pb=pb: h.tensor_copy(part_b[pb][:], ps_w[:, :]),
                                ("ps_w",), (f"partb{pb}",), name=f"pev{tt}")
                            dma_sp(sp_part[tt, :, :], part_b[pb][:], (f"partb{pb}",),
                                   (f"sp_part{tt}",), name=f"pw{tt}")
                        else:
                            dma_sp(part_b[pb][:], sp_part[tt, :, :], (f"sp_part{tt}",),
                                   (f"partb{pb}",), name=f"pr{tt}")
                            dma_sp(xres[pb][:], xin[3 + tt * 128: 3 + (tt + 1) * 128, :],
                                   (), (f"xres{pb}",), name=f"xrr{tt}")
                            dma_sp(rcol[:], bass.AP(r_d.tensor, tt * CH, [[1, 128], [1, 1]]),
                                   ("r_d",), ("rcol",), name=f"rcl{tt}")
                            dve(lambda h, pb=pb: h.tensor_tensor(h1_sb[:], part_b[pb][:],
                                                                 ps_w[:, :], ALU.add),
                                (f"partb{pb}", "ps_w"), ("h1_sb",), name=f"h1a{tt}")
                            dve(lambda h, pb=pb: h.scalar_tensor_tensor(
                                    h1_sb[:], h1_sb[:], rcol[:, 0:1], xres[pb][:],
                                    ALU.mult, ALU.add),
                                ("h1_sb", "rcol", f"xres{pb}"), ("h1_sb",), name=f"h1b{tt}")
                            dma_sp(sp_h1[tt, :, :], h1_sb[:], ("h1_sb",), (f"sp_h1{tt}",),
                                   name=f"h1w{tt}")
                            if dv_h1 is not None:
                                dma_sp(dv_h1[tt, :, :], h1_sb[:], ("h1_sb",), ())
                            ln_stats_apply(h1_sb, "h1_sb", x2t[:], "x2t")
                            if dv_x2 is not None:
                                dve(lambda h: h.tensor_copy(scratch_f[:], x2t[:]),
                                    ("x2t",), ("scratch_f",))
                                dma_sp(dv_x2[tt, :, :], scratch_f[:], ("scratch_f",), ())
                            for kg in range(2):
                                fns = []
                                for i in range(8):
                                    kt = kg * 8 + i
                                    def f(h, kt=kt, i=i):
                                        return h.transpose(ps_tx[:, i * 128:(i + 1) * 128],
                                                           x2t[:, kt * 128:(kt + 1) * 128],
                                                           ident_bf[:])
                                    fns.append(f)
                                kb.group("pe", fns, ("x2t", "ident_bf"), ("ps_tx",),
                                         name=f"x2tr{tt}_{kg}")
                                def ev(h, tt=tt, kg=kg):
                                    return h.tensor_copy(
                                        bass.AP(x2_fm, kg * 8 * T + tt * 128,
                                                [[NKT * T, 128], [T, 8], [1, 128]]),
                                        bass.AP(ps_tx, 0, [[1024, 128], [128, 8], [1, 128]]))
                                dve(ev, ("ps_tx",), ("x2_fm",), name=f"x2ev{tt}_{kg}")
            kb.barrier()

        if stage < 6:
            _fin(nc, kb, ctx)
            return nc, dev_outs

        with ExitStack() as sc4:
            E4 = sc4.enter_context
            stk_rt = ExitStack()
            ER = stk_rt.enter_context
            rw_sb = ER(nc.sbuf_tensor("rw_sb", [8, HID], F32))
            rwT_bf = ER(nc.sbuf_tensor("rwT_bf", [128, NKT * 8], BF16))
            sc_tm = ER(nc.sbuf_tensor("sc_tm", [128, NKT * 8], F32))
            w_tm = ER(nc.sbuf_tensor("w_tm", [128, NKT * 8], F32))
            w_fm = ER(nc.sbuf_tensor("w_fm", [8, T], F32))
            t8a = ER(nc.sbuf_tensor("t8a", [128, 8], F32))
            t8b = ER(nc.sbuf_tensor("t8b", [128, 8], F32))
            mx1 = ER(nc.sbuf_tensor("mx1", [128, 1], F32))
            mx2 = ER(nc.sbuf_tensor("mx2", [128, 1], F32))
            p1c = ER(nc.sbuf_tensor("p1c", [128, 1], F32))
            p2c = ER(nc.sbuf_tensor("p2c", [128, 1], F32))
            dma_sp(rw_sb[:], rwt[:, :], (), ("rw_sb",))
            with nc.psum_tensor("ps_r", [128, 512], F32) as ps_r, \
                 nc.psum_tensor("ps_s", [8, T], F32) as ps_s:
                for kt in range(NKT):
                    pe1(lambda h, kt=kt: h.transpose(ps_r[:, 0:8],
                                                     rw_sb[0:8, kt * 128:(kt + 1) * 128],
                                                     ident_f[0:8, 0:8]),
                        ("rw_sb", "ident_f"), ("ps_r",), name=f"rwtr{kt}")
                    dve(lambda h, kt=kt: h.tensor_copy(rwT_bf[:, kt * 8:(kt + 1) * 8],
                                                       ps_r[:, 0:8]),
                        ("ps_r",), ("rwT_bf",), name=f"rwev{kt}")
                fns = []
                for kt in range(NKT):
                    for nb in range(4):
                        def mm(h, kt=kt, nb=nb):
                            return h.matmul(ps_s[0:8, nb * 512:(nb + 1) * 512],
                                            rwT_bf[:, kt * 8:(kt + 1) * 8],
                                            x2_fm[:, kt * T + nb * 512: kt * T + (nb + 1) * 512],
                                            start=(kt == 0), stop=(kt == NKT - 1))
                        fns.append(mm)
                kb.group("pe", fns, ("rwT_bf", "x2_fm"), ("ps_s",), name="scmm")
                dve(lambda h: h.tensor_copy(w_fm[:], ps_s[:, :]), ("ps_s",), ("w_fm",),
                    name="scev")
                for tt in range(NCH):
                    pe1(lambda h, tt=tt: h.transpose(ps_r[:, 0:8],
                                                     w_fm[0:8, tt * 128:(tt + 1) * 128],
                                                     ident_f[0:8, 0:8]),
                        ("w_fm", "ident_f"), ("ps_r",), name=f"sctr{tt}")
                    dve(lambda h, tt=tt: h.tensor_copy(sc_tm[:, tt * 8:(tt + 1) * 8],
                                                       ps_r[:, 0:8]),
                        ("ps_r",), ("sc_tm",), name=f"scev{tt}")
                    if dv_sc is not None:
                        dve(lambda h, tt=tt: h.tensor_copy(scratch_f[:, 0:8],
                                                           sc_tm[:, tt * 8:(tt + 1) * 8]),
                            ("sc_tm",), ("scratch_f",))
                        dma_sp(dv_sc[tt, :, :], scratch_f[:, 0:8], ("scratch_f",), ())
                for tt in range(NCH):
                    s8 = sc_tm[:, tt * 8:(tt + 1) * 8]
                    dve(lambda h, s8=s8: h.reduce_max(mx1[:], s8, AX.X), ("sc_tm",), ("mx1",))
                    dve(lambda h, s8=s8: h.tensor_scalar(t8a[:], s8, mx1[:, 0:1], None,
                                                         ALU.is_ge),
                        ("sc_tm", "mx1"), ("t8a",))
                    dve(lambda h, s8=s8: h.scalar_tensor_tensor(t8b[:], t8a[:], -1e30,
                                                                s8, ALU.mult, ALU.add),
                        ("t8a", "sc_tm"), ("t8b",))
                    dve(lambda h: h.reduce_max(mx2[:], t8b[:], AX.X), ("t8b",), ("mx2",))
                    dve(lambda h, s8=s8: h.tensor_scalar(t8b[:], s8, mx2[:, 0:1], None,
                                                         ALU.is_ge),
                        ("sc_tm", "mx2"), ("t8b",))
                    dve(lambda h: h.tensor_sub(v_c[:], mx1[:], mx2[:]), ("mx1", "mx2"),
                        ("v_c",))
                    act(lambda h: h.activation(p1c[:], v_c[:], AF.Sigmoid), ("v_c",), ("p1c",))
                    dve(lambda h: h.tensor_scalar(p2c[:], p1c[:], -1.0, 1.0, ALU.mult,
                                                  ALU.add), ("p1c",), ("p2c",))
                    dve(lambda h: h.tensor_sub(t8b[:], t8b[:], t8a[:]), ("t8b", "t8a"),
                        ("t8b",))
                    dve(lambda h: h.tensor_scalar(t8a[:], t8a[:], p1c[:, 0:1], None,
                                                  ALU.mult), ("t8a", "p1c"), ("t8a",))
                    dve(lambda h: h.tensor_scalar(t8b[:], t8b[:], p2c[:, 0:1], None,
                                                  ALU.mult), ("t8b", "p2c"), ("t8b",))
                    dve(lambda h, tt=tt: h.tensor_add(w_tm[:, tt * 8:(tt + 1) * 8],
                                                      t8a[:], t8b[:]),
                        ("t8a", "t8b"), ("w_tm",), name=f"wall{tt}")
                    if dv_wall is not None:
                        dve(lambda h, tt=tt: h.tensor_copy(scratch_f[:, 0:8],
                                                           w_tm[:, tt * 8:(tt + 1) * 8]),
                            ("w_tm",), ("scratch_f",))
                        dma_sp(dv_wall[tt, :, :], scratch_f[:, 0:8], ("scratch_f",), ())
                for tt in range(NCH):
                    pe1(lambda h, tt=tt: h.transpose(ps_r[0:8, 0:128],
                                                     w_tm[:, tt * 8:(tt + 1) * 8],
                                                     ident_f[:]),
                        ("w_tm", "ident_f"), ("ps_r",), name=f"wtr{tt}")
                    dve(lambda h, tt=tt: h.tensor_copy(w_fm[0:8, tt * 128:(tt + 1) * 128],
                                                       ps_r[0:8, 0:128]),
                        ("ps_r",), ("w_fm",), name=f"wfm{tt}")
                dma_sp(wrow_d[:, :], w_fm[:], ("w_fm",), ("wrow_d",), name="wrw")
            kb.barrier()
            stk_rt.close()

            for gi in range(2):
                stk_h = ExitStack()
                EH = stk_h.enter_context
                w1_sb = [EH(nc.sbuf_tensor(f"w1s{i}_{gi}", [128, NKT * 512], BF16)) for i in range(2)]
                w3_sb = [EH(nc.sbuf_tensor(f"w3s{i}_{gi}", [128, NKT * 512], BF16)) for i in range(2)]
                wb_sb = EH(nc.sbuf_tensor(f"wb_sb{gi}", [128, 512], F32))
                wr_sb = EH(nc.sbuf_tensor(f"wr_sb{gi}", [1, 512], F32))
                g_sb = EH(nc.sbuf_tensor(f"g_sb{gi}", [128, 2048], BF16))
                s1_f = EH(nc.sbuf_tensor(f"s1_f{gi}", [128, 512], F32))
                s2_f = EH(nc.sbuf_tensor(f"s2_f{gi}", [128, 512], F32))

                def load_w13(e, buf_i, which, wt):
                    tgt = w1_sb[buf_i] if which == 1 else w3_sb[buf_i]
                    dst = bass.AP(tgt, 0, [[NKT * 512, 128], [512, NKT], [1, 512]])
                    src = bass.AP(wt.tensor, e * HID * 512,
                                  [[512, 128], [512 * 128, NKT], [1, 512]])
                    dma_pool(dst, src, (), (f"w{which}s{buf_i}",), name=f"w{which}ld{e}")

                with nc.psum_tensor(f"ps_h1_{gi}", [128, 2048], F32) as ps_h1, \
                     nc.psum_tensor(f"ps_h3_{gi}", [128, 2048], F32) as ps_h3:
                    load_w13(gi * 4, 0, 1, w1)
                    load_w13(gi * 4, 0, 3, w3)
                    for ei in range(4):
                        e = gi * 4 + ei
                        bi = ei % 2
                        if ei + 1 < 4:
                            load_w13(e + 1, (ei + 1) % 2, 1, w1)
                            load_w13(e + 1, (ei + 1) % 2, 3, w3)
                        for tb in range(4):
                            dma_sp(wr_sb[:], bass.AP(wrow_d.tensor, e * T + tb * 512,
                                                     [[1, 1], [1, 512]]),
                                   ("wrow_d",), ("wr_sb",), name=f"wrr{e}_{tb}")
                            pe1(lambda h: h.matmul(ps_h1[:, 0:512], ones1[:, :],
                                                   wr_sb[:, :], start=True, stop=True),
                                ("wr_sb", "ones1"), ("ps_h1",), name=f"wbb{e}_{tb}")
                            dve(lambda h: h.tensor_copy(wb_sb[:], ps_h1[:, 0:512]),
                                ("ps_h1",), ("wb_sb",), name=f"wbe{e}_{tb}")
                            for which, wsb, wkey in ((1, w1_sb, f"w1s{bi}"),
                                                     (3, w3_sb, f"w3s{bi}")):
                                psh = ps_h1 if which == 1 else ps_h3
                                phk = "ps_h1" if which == 1 else "ps_h3"
                                fns = []
                                for ft in range(4):
                                    for kt in range(NKT):
                                        def mm(h, kt=kt, ft=ft, tb=tb, bi=bi, psh=psh,
                                               wsb=wsb):
                                            return h.matmul(
                                                psh[:, ft * 512:(ft + 1) * 512],
                                                wsb[bi][:, kt * 512 + ft * 128: kt * 512 + (ft + 1) * 128],
                                                x2_fm[:, kt * T + tb * 512: kt * T + (tb + 1) * 512],
                                                start=(kt == 0), stop=(kt == NKT - 1))
                                        fns.append(mm)
                                kb.group("pe", fns, (wkey, "x2_fm"), (phk,),
                                         name=f"h{which}_{e}_{tb}")
                            for ft in range(4):
                                act(lambda h, ft=ft: h.activation(
                                        s1_f[:], ps_h1[:, ft * 512:(ft + 1) * 512], AF.Silu),
                                    ("ps_h1",), ("s1_f",), name=f"msl{e}_{tb}_{ft}")
                                dve(lambda h, ft=ft: h.tensor_tensor(
                                        s2_f[:], s1_f[:], ps_h3[:, ft * 512:(ft + 1) * 512],
                                        ALU.mult),
                                    ("s1_f", "ps_h3"), ("s2_f",), name=f"mg1{e}_{tb}_{ft}")
                                dve(lambda h, ft=ft: h.tensor_tensor(
                                        g_sb[:, ft * 512:(ft + 1) * 512], s2_f[:], wb_sb[:],
                                        ALU.mult),
                                    ("s2_f", "wb_sb"), ("g_sb",), name=f"mg2{e}_{tb}_{ft}")
                            dma_sp(bass.AP(sp_g.tensor, ei * 4 * 128 * T + tb * 512,
                                           [[T, 128], [128 * T, 4], [1, 512]]),
                                   bass.AP(g_sb, 0, [[2048, 128], [512, 4], [1, 512]]),
                                   ("g_sb",), (f"sp_g{ei}_{tb}",), name=f"gw{e}_{tb}")
                kb.barrier()
                stk_h.close()
                stk_w2 = ExitStack()
                EW2 = stk_w2.enter_context
                w2g_sb = EW2(nc.sbuf_tensor(f"w2g_sb{gi}", [128, 16 * HID], BF16))
                gt_buf = [EW2(nc.sbuf_tensor(f"gt{i}_{gi}", [128, 16 * 128], BF16)) for i in range(2)]
                moe_b = [EW2(nc.sbuf_tensor(f"moe{i}_{gi}", [128, HID], F32)) for i in range(2)]
                h1r = [EW2(nc.sbuf_tensor(f"h1r{i}_{gi}", [128, HID], F32)) for i in range(2)]
                with nc.psum_tensor(f"ps_o_{gi}", [128, HID], F32) as ps_o:
                    for ei in range(4):
                        e = gi * 4 + ei
                        for kt4 in range(4):
                            dma_pool(bass.AP(w2g_sb, (ei * 4 + kt4) * HID,
                                             [[16 * HID, 128], [1, HID]]),
                                     bass.AP(w2.tensor, e * 512 * HID + kt4 * 128 * HID,
                                             [[HID, 128], [1, HID]]),
                                     (), (f"w2g{ei}_{kt4}",), name=f"w2ld{e}_{kt4}")
                    for tt in range(NCH):
                        pb = tt % 2
                        dma_sp(bass.AP(gt_buf[pb], 0, [[16 * 128, 128], [128, 16], [1, 128]]),
                               bass.AP(sp_g.tensor, tt * 128,
                                       [[T, 128], [128 * T, 16], [1, 128]]),
                               tuple(f"sp_g{ei}_{tt // 4}" for ei in range(4)),
                               (f"gt{pb}",), name=f"gr{gi}_{tt}")
                        fns = []
                        for j in range(16):
                            for nb in range(4):
                                def mm(h, j=j, nb=nb, pb=pb):
                                    return h.matmul(
                                        ps_o[:, nb * 512:(nb + 1) * 512],
                                        gt_buf[pb][:, j * 128:(j + 1) * 128],
                                        w2g_sb[:, j * HID + nb * 512: j * HID + (nb + 1) * 512],
                                        start=(j == 0), stop=(j == 15))
                                fns.append(mm)
                        kb.group("pe", fns,
                                 tuple([f"gt{pb}"] +
                                       [f"w2g{ei}_{k4}" for ei in range(4) for k4 in range(4)]),
                                 ("ps_o",), name=f"w2mm{gi}_{tt}")
                        if gi == 0:
                            dve(lambda h, pb=pb: h.tensor_copy(moe_b[pb][:], ps_o[:, :]),
                                ("ps_o",), (f"moeb{pb}",), name=f"moev{tt}")
                            dma_sp(sp_part[tt, :, :], moe_b[pb][:], (f"moeb{pb}",),
                                   (f"sp_part{tt}",), name=f"mow{tt}")
                        else:
                            dma_sp(moe_b[pb][:], sp_part[tt, :, :], (f"sp_part{tt}",),
                                   (f"moeb{pb}",), name=f"mor{tt}")
                            dma_sp(h1r[pb][:], sp_h1[tt, :, :], (f"sp_h1{tt}",),
                                   (f"h1r{pb}",), name=f"h1rr{tt}")
                            dve(lambda h, pb=pb: h.tensor_tensor(moe_b[pb][:],
                                                                 moe_b[pb][:], ps_o[:, :],
                                                                 ALU.add),
                                (f"moeb{pb}", "ps_o"), (f"moeb{pb}",), name=f"oa{tt}")
                            dve(lambda h, pb=pb: h.tensor_tensor(moe_b[pb][:],
                                                                 moe_b[pb][:], h1r[pb][:],
                                                                 ALU.add),
                                (f"moeb{pb}", f"h1r{pb}"), (f"moeb{pb}",), name=f"ob{tt}")
                            dma_sp(out[tt * 128:(tt + 1) * 128, :], moe_b[pb][:],
                                   (f"moeb{pb}",), ("OUT",), name=f"outw{tt}")
                kb.barrier()
                stk_w2.close()

        _fin(nc, kb, ctx)
    return nc, dev_outs


def _fin(nc, kb, ctx):
    kb.barrier()
    sems = {e: ctx.enter_context(nc.semaphore("sem_" + e)) for e in ENGS}
    block = ctx.enter_context(nc.Block())
    kb.finalize(block, sems)


def make_in_maps(inputs):
    x = np.asarray(inputs["L0_attn_hidden_states"])
    base = {
        "win": np.ascontiguousarray(inputs["L0_attn_W_in"]),
        "convw": np.ascontiguousarray(inputs["L0_attn_conv_w"]),
        "convb": np.ascontiguousarray(inputs["L0_attn_conv_b"]),
        "dtb": np.ascontiguousarray(inputs["L0_attn_dt_bias"]),
        "alog": np.ascontiguousarray(inputs["L0_attn_A_log"]),
        "dvec": np.ascontiguousarray(inputs["L0_attn_D"]),
        "nwt": np.ascontiguousarray(inputs["L0_attn_norm_weight"]),
        "wout": np.ascontiguousarray(inputs["L0_attn_W_out"]),
        "rwt": np.ascontiguousarray(inputs["L0_ffn_router_w"]),
        "w1": np.ascontiguousarray(inputs["L0_ffn_w1"]),
        "w2": np.ascontiguousarray(inputs["L0_ffn_w2"]),
        "w3": np.ascontiguousarray(inputs["L0_ffn_w3"]),
    }
    maps = []
    for core in range(8):
        b, hh = core // 2, core % 2
        xs = np.zeros((TH, HID), np.float32)
        if hh == 0:
            xs[3:] = x[b, 0:T]
        else:
            xs[:] = x[b, T - 3: 2 * T]
        m = dict(base)
        m["xin"] = xs
        m["smask"] = np.full((128, 1), float(hh), np.float32)
        maps.append(m)
    return maps


def kernel(**inputs):
    nc, _ = build_program()
    maps = make_in_maps(inputs)
    res = run_bass_kernel_spmd(nc, maps, core_ids=list(range(8)))
    outb = np.zeros((4, 2 * T, HID), np.float32)
    for core in range(8):
        b, hh = core // 2, core % 2
        outb[b, hh * T:(hh + 1) * T] = res.results[core]["out"]
    return outb


# version 02
# speedup vs baseline: 1.0329x; 1.0329x over previous
import numpy as np
import ml_dtypes
from contextlib import ExitStack

import concourse.bass as bass
import concourse.mybir as mybir
from concourse.bass_utils import run_bass_kernel_spmd

F32 = mybir.dt.float32
BF16 = mybir.dt.bfloat16
AF = mybir.ActivationFunctionType
ALU = mybir.AluOpType
AX = mybir.AxisListType

HID = 2048
OUT_IN = 8512
INTER = 4096
H = 64
P = 64
KCONV = 4
TH = 2051
T = 2048
NCH = 16
CH = 128
NKT = 16
EPS_LN = 1e-5
EPS_RMS = 1e-5
DT_MAX = 100.0
XT0 = 32
BT_ = 64
CT_ = 65
DTT = 66

ENGS = ("pe", "dve", "act", "pool", "sp")


class KB:
    def __init__(self):
        self.ops = {e: [] for e in ENGS}
        self.cnt = {e: 0 for e in ENGS}
        self.last_w = {}
        self.readers = {}
        self.water = {e: {c: 0 for c in ENGS} for e in ENGS}

    def emit(self, eng, fn, reads=(), writes=(), inc=1, name=""):
        waits = {}
        def want(ch, c):
            if ch == "pe" and eng == "pe":
                return
            if c > waits.get(ch, 0):
                waits[ch] = c
        for k in reads:
            lw = self.last_w.get(k)
            if lw is not None:
                want(*lw)
        for k in writes:
            lw = self.last_w.get(k)
            if lw is not None:
                want(*lw)
            for ch, c in self.readers.get(k, {}).items():
                want(ch, c)
        w2 = {ch: c for ch, c in waits.items() if c > self.water[eng][ch]}
        for ch, c in w2.items():
            self.water[eng][ch] = c
        self.cnt[eng] += inc
        after = self.cnt[eng]
        self.ops[eng].append((w2, fn, inc, name))
        for k in writes:
            self.last_w[k] = (eng, after)
            self.readers[k] = {}
        for k in reads:
            rd = self.readers.setdefault(k, {})
            if rd.get(eng, 0) < after:
                rd[eng] = after
        return after

    def group(self, eng, fns, reads=(), writes=(), inc=1, name=""):
        def run(h):
            last = None
            for f in fns:
                last = f(h)
            return last
        return self.emit(eng, run, reads, writes, inc, name)

    def barrier(self):
        snap = dict(self.cnt)
        for e in ENGS:
            w = {}
            for ch, c in snap.items():
                if ch != e and c > self.water[e][ch]:
                    self.water[e][ch] = c
                    w[ch] = c
            if w:
                self.ops[e].append((w, None, 0, "barrier"))

    def finalize(self, block, sems):
        def section(eng):
            def body(h):
                for waits, fn, inc, name in self.ops[eng]:
                    for ch, c in waits.items():
                        h.wait_ge(sems[ch], c)
                    if fn is not None:
                        inst = fn(h)
                        if inc:
                            inst.then_inc(sems[eng], inc)
            return body
        block.tensor(section("pe"))
        block.vector(section("dve"))
        block.scalar(section("act"))
        block.gpsimd(section("pool"))
        block.sync(section("sp"))


def build_program(dev=(), stage=99):
    nc = bass.Bass("TRN2", target_bir_lowering=False)
    DEV = set(dev)
    kb = KB()

    xin = nc.dram_tensor("xin", [TH, HID], F32, kind="ExternalInput").ap()
    win = nc.dram_tensor("win", [HID, OUT_IN], F32, kind="ExternalInput").ap()
    convw = nc.dram_tensor("convw", [KCONV, 4352], F32, kind="ExternalInput").ap()
    convb = nc.dram_tensor("convb", [4352], F32, kind="ExternalInput").ap()
    dtb = nc.dram_tensor("dtb", [H], F32, kind="ExternalInput").ap()
    alog = nc.dram_tensor("alog", [H], F32, kind="ExternalInput").ap()
    dvec = nc.dram_tensor("dvec", [H], F32, kind="ExternalInput").ap()
    nwt = nc.dram_tensor("nwt", [INTER], F32, kind="ExternalInput").ap()
    wout = nc.dram_tensor("wout", [INTER, HID], F32, kind="ExternalInput").ap()
    rwt = nc.dram_tensor("rwt", [8, HID], F32, kind="ExternalInput").ap()
    w1 = nc.dram_tensor("w1", [8, HID, 512], F32, kind="ExternalInput").ap()
    w2 = nc.dram_tensor("w2", [8, 512, HID], F32, kind="ExternalInput").ap()
    w3 = nc.dram_tensor("w3", [8, HID, 512], F32, kind="ExternalInput").ap()
    smask = nc.dram_tensor("smask", [128, 1], F32, kind="ExternalInput").ap()
    out = nc.dram_tensor("out", [T, HID], F32, kind="ExternalOutput").ap()

    sp_xtm = nc.dram_tensor("sp_xtm", [NCH, 128, INTER], BF16).ap()
    sp_z = nc.dram_tensor("sp_z", [32, 128, T], BF16).ap()
    sp_S = nc.dram_tensor("sp_S", [NCH, 128, INTER], BF16).ap()
    sp_yf = nc.dram_tensor("sp_yf", [NCH, 32, 128, CH], BF16).ap()
    sp_h1 = nc.dram_tensor("sp_h1", [NCH, 128, HID], F32).ap()
    sp_part = nc.dram_tensor("sp_part", [NCH, 128, HID], F32).ap()
    sp_g = nc.dram_tensor("sp_g", [4, 4, 128, T], BF16).ap()
    gflat_d = nc.dram_tensor("gflat_d", [NCH, H, CH], F32).ap()
    dce_exp = nc.dram_tensor("dce_exp", [NCH, INTER], F32).ap()
    r_d = nc.dram_tensor("r_d", [T], F32).ap()
    wrow_d = nc.dram_tensor("wrow_d", [8, T], F32).ap()
    cc_in = nc.dram_tensor("cc_in", [128, INTER], F32)
    cc_out = nc.dram_tensor("cc_out", [256, INTER], F32)

    dev_outs = {}
    def devt(name, shape):
        if name in DEV:
            dev_outs[name] = nc.dram_tensor("dev_" + name, shape, F32,
                                            kind="ExternalOutput").ap()
            return dev_outs[name]
        return None

    dv_zx = devt("zx", [67, 128, T])
    dv_conv = devt("conv", [34, 128, T])
    dv_dt = devt("dtp", [4, H, T])
    dv_S = devt("Ssp", [NCH, 128, INTER])
    dv_y = devt("y", [NCH, 128, INTER])
    dv_yf = devt("yf", [NCH, 32, 128, CH])
    dv_rd = devt("rd", [NCH, CH])
    dv_yfsp = devt("yfsp", [NCH, 32, 128, CH])
    dv_wo = devt("wo", [2, 16, 128, HID])
    dv_h1 = devt("h1", [NCH, 128, HID])
    dv_x2 = devt("x2", [NCH, 128, HID])
    dv_sc = devt("scores", [NCH, 128, 8])
    dv_wall = devt("wall", [NCH, 128, 8])

    with ExitStack() as ctx:
        E = ctx.enter_context

        acol = E(nc.sbuf_tensor("acol", [H, 1], F32))
        dtb_sb = E(nc.sbuf_tensor("dtb_sb", [H, 1], F32))
        alog_sb = E(nc.sbuf_tensor("alog_sb", [H, 1], F32))
        convb_sb = E(nc.sbuf_tensor("convb_sb", [128, 34], F32))
        nw_sb = E(nc.sbuf_tensor("nw_sb", [128, 32], F32))
        ident_bf = E(nc.sbuf_tensor("ident_bf", [128, 128], BF16))
        ident_f = E(nc.sbuf_tensor("ident_f", [128, 128], F32))
        triu_f = E(nc.sbuf_tensor("triu_f", [128, 128], F32))
        ones1 = E(nc.sbuf_tensor("ones1", [1, 128], F32))
        ones128_bf = E(nc.sbuf_tensor("ones128_bf", [128, 1], BF16))
        zero_col = E(nc.sbuf_tensor("zero_col", [128, 1], F32))
        smask_sb = E(nc.sbuf_tensor("smask_sb", [128, 1], F32))
        m_c = E(nc.sbuf_tensor("m_c", [128, 1], F32))
        sq_c = E(nc.sbuf_tensor("sq_c", [128, 1], F32))
        v_c = E(nc.sbuf_tensor("v_c", [128, 1], F32))
        r_c = E(nc.sbuf_tensor("r_c", [128, 1], F32))
        nmr_c = E(nc.sbuf_tensor("nmr_c", [128, 1], F32))
        scratch_f = E(nc.sbuf_tensor("scratch_f", [128, HID], F32))

        def dve(fn, reads=(), writes=(), name=""):
            kb.emit("dve", fn, reads, writes, 1, name)

        def act(fn, reads=(), writes=(), name=""):
            kb.emit("act", fn, reads, writes, 1, name)

        def pe1(fn, reads=(), writes=(), name=""):
            kb.emit("pe", fn, reads, writes, 1, name)

        def pool(fn, reads=(), writes=(), name=""):
            kb.emit("pool", fn, reads, writes, 1, name)

        def dma_sp(dst, src, reads=(), writes=(), name=""):
            kb.emit("sp", lambda h: h.dma_start(dst, src), reads, writes, 16, name)

        def dma_pool(dst, src, reads=(), writes=(), name=""):
            kb.emit("pool", lambda h: h.dma_start(dst, src), reads, writes, 16, name)

        def ln_stats_apply(src_sb, src_key, dst_ap, dst_key, rows=128):
            dve(lambda h: h.reduce_sum(m_c[0:rows, :], src_sb[0:rows, :], AX.X),
                (src_key,), ("m_c",))
            act(lambda h: h.activation(scratch_f[0:rows, :], src_sb[0:rows, :],
                                       AF.Square, accum_out=sq_c[0:rows, 0:1]),
                (src_key,), ("sq_c", "scratch_f"))
            dve(lambda h: h.tensor_scalar_mul(m_c[0:rows, :], m_c[0:rows, :], 1.0 / HID),
                ("m_c",), ("m_c",))
            dve(lambda h: h.tensor_scalar_mul(sq_c[0:rows, :], sq_c[0:rows, :], 1.0 / HID),
                ("sq_c",), ("sq_c",))
            dve(lambda h: h.tensor_tensor(v_c[0:rows, :], m_c[0:rows, :], m_c[0:rows, :],
                                          ALU.mult), ("m_c",), ("v_c",))
            dve(lambda h: h.tensor_sub(v_c[0:rows, :], sq_c[0:rows, :], v_c[0:rows, :]),
                ("sq_c", "v_c"), ("v_c",))
            dve(lambda h: h.tensor_scalar_add(v_c[0:rows, :], v_c[0:rows, :], EPS_LN),
                ("v_c",), ("v_c",))
            act(lambda h: h.activation(r_c[0:rows, :], v_c[0:rows, :], AF.Sqrt),
                ("v_c",), ("r_c",))
            dve(lambda h: h.reciprocal(r_c[0:rows, :], r_c[0:rows, :]), ("r_c",), ("r_c",))
            dve(lambda h: h.scalar_tensor_tensor(nmr_c[0:rows, :], m_c[0:rows, :], -1.0,
                                                 r_c[0:rows, :], ALU.mult, ALU.mult),
                ("m_c", "r_c"), ("nmr_c",))
            dve(lambda h: h.tensor_scalar(dst_ap, src_sb[0:rows, :], r_c[0:rows, 0:1],
                                          nmr_c[0:rows, 0:1], ALU.mult, ALU.add),
                (src_key, "r_c", "nmr_c"), (dst_key,))

        colidx = E(nc.sbuf_tensor("colidx", [128, 128], F32))
        partidx = E(nc.sbuf_tensor("partidx", [128, 128], F32))
        pool(lambda h: h.iota(colidx[:], [[1, 128]], channel_multiplier=0,
                              allow_small_or_imprecise_dtypes=True), (), ("colidx",))
        pool(lambda h: h.iota(partidx[:], [[0, 128]], channel_multiplier=1,
                              allow_small_or_imprecise_dtypes=True), (), ("partidx",))
        pool(lambda h: h.memset(ones1[:], 1.0), (), ("ones1",))
        pool(lambda h: h.memset(ones128_bf[:], 1.0), (), ("ones128_bf",))
        pool(lambda h: h.memset(zero_col[:], 0.0), (), ("zero_col",))
        dve(lambda h: h.tensor_tensor(ident_f[:], colidx[:], partidx[:], ALU.is_equal),
            ("colidx", "partidx"), ("ident_f",))
        dve(lambda h: h.tensor_copy(ident_bf[:], ident_f[:]), ("ident_f",), ("ident_bf",))
        dve(lambda h: h.tensor_tensor(triu_f[:], colidx[:], partidx[:], ALU.is_ge),
            ("colidx", "partidx"), ("triu_f",))
        dma_sp(dtb_sb[:, 0:1], bass.AP(dtb.tensor, 0, [[1, H], [1, 1]]), (), ("dtb_sb",))
        dma_sp(alog_sb[:, 0:1], bass.AP(alog.tensor, 0, [[1, H], [1, 1]]), (), ("alog_sb",))
        dma_sp(smask_sb[:], smask[:, :], (), ("smask_sb",))
        kb.emit("sp", lambda h: h.dma_start(convb_sb[:],
                bass.AP(convb.tensor, 0, [[1, 128], [128, 34]]),
                allow_slow_non_contiguous=True), (), ("convb_sb",), 16, "convb")
        kb.emit("sp", lambda h: h.dma_start(nw_sb[:],
                bass.AP(nwt.tensor, 0, [[1, 128], [128, 32]]),
                allow_slow_non_contiguous=True), (), ("nw_sb",), 16, "nw")
        act(lambda h: h.activation(acol[:], alog_sb[:], AF.Exp), ("alog_sb",), ("acol",))
        dve(lambda h: h.tensor_scalar_mul(acol[:], acol[:], -1.0), ("acol",), ("acol",))
        stk_ab = ExitStack()
        ctx.callback(stk_ab.close)
        EA = stk_ab.enter_context
        B_fm = EA(nc.sbuf_tensor("B_fm", [128, T], BF16))
        C_fm = EA(nc.sbuf_tensor("C_fm", [128, T], BF16))
        B_tm = EA(nc.sbuf_tensor("B_tm", [128, T], BF16))
        dt_bf = EA(nc.sbuf_tensor("dt_bf", [H, T], BF16))
        sdec_bf = EA(nc.sbuf_tensor("sdec_bf", [H, T], BF16))
        dtdec_bf = EA(nc.sbuf_tensor("dtdec_bf", [H, T], BF16))
        dsel_bf = EA(nc.sbuf_tensor("dsel_bf", [H, INTER], BF16))
        xtm_buf = [EA(nc.sbuf_tensor(f"xtm{i}", [128, INTER], BF16)) for i in range(2)]

        with ExitStack() as sc1:
            E1 = sc1.enter_context
            xlnT = E1(nc.sbuf_tensor("xlnT", [128, NKT * TH], BF16))
            dt_raw = E1(nc.sbuf_tensor("dt_raw", [H, T], F32))

            with nc.sbuf_tensor("hsel", [H, INTER], F32) as hsel, \
                 nc.sbuf_tensor("hpart", [H, INTER], F32) as hpart:
                pool(lambda h: h.iota(hsel[:], [[1, H], [0, P]], channel_multiplier=0,
                                      allow_small_or_imprecise_dtypes=True), (), ("hsel",))
                pool(lambda h: h.iota(hpart[:], [[0, H], [0, P]], channel_multiplier=1,
                                      allow_small_or_imprecise_dtypes=True), (), ("hpart",))
                dve(lambda h: h.tensor_tensor(hsel[:], hsel[:], hpart[:], ALU.is_equal),
                    ("hsel", "hpart"), ("hsel",))
                dve(lambda h: h.tensor_copy(dsel_bf[:], hsel[:]), ("hsel",), ("dsel_bf",))
                kb.barrier()

            stk_p0 = ExitStack()
            ln_bf = stk_p0.enter_context(nc.sbuf_tensor("ln_bf", [128, HID], BF16))
            xt_ = [stk_p0.enter_context(nc.sbuf_tensor(f"xt{i}", [128, HID], F32))
                   for i in range(2)]
            with nc.psum_tensor("ps_t0", [128, 1024], BF16) as ps_t0, \
                 nc.psum_tensor("ps_t1", [128, 1024], BF16) as ps_t1:
                for tt in range(17):
                    rows = 128 if tt < 16 else 3
                    pb = tt % 2
                    xk = f"xt{pb}"
                    dma_sp(xt_[pb][0:rows, :], xin[tt * 128: tt * 128 + rows, :], (), (xk,),
                           name=f"xld{tt}")
                    ln_stats_apply(xt_[pb], xk, ln_bf[0:rows, :], "ln_bf", rows)
                    for kg in range(2):
                        psq = ps_t0 if kg == 0 else ps_t1
                        pk = f"ps_t{kg}"
                        fns = []
                        for i in range(8):
                            kt = kg * 8 + i
                            def f(h, kt=kt, i=i, psq=psq, rows=rows):
                                return h.transpose(psq[:, i * 128: i * 128 + rows],
                                                   ln_bf[0:rows, kt * 128:(kt + 1) * 128],
                                                   ident_bf[0:rows, 0:rows])
                            fns.append(f)
                        kb.group("pe", fns, ("ln_bf", "ident_bf"), (pk,), name=f"lntr{tt}_{kg}")
                        def ev(h, kg=kg, tt=tt, psq=psq, rows=rows):
                            return h.tensor_copy(
                                bass.AP(xlnT, kg * 8 * TH + tt * 128,
                                        [[NKT * TH, 128], [TH, 8], [1, rows]]),
                                bass.AP(psq, 0, [[1024, 128], [128, 8], [1, rows]]))
                        dve(ev, (pk,), ("xlnT",), name=f"lnev{tt}_{kg}")
            kb.barrier()
            stk_p0.close()

            if stage < 1:
                _fin(nc, kb, ctx)
                return nc, dev_outs

            stk_p1 = ExitStack()
            EP1 = stk_p1.enter_context
            wblk = [EP1(nc.sbuf_tensor(f"wblk{i}", [128, NKT * 128], BF16)) for i in range(2)]
            raw_sb = EP1(nc.sbuf_tensor("raw_sb", [128, TH], BF16))
            xconv_sb = EP1(nc.sbuf_tensor("xconv_sb", [128, T], BF16))
            stage_sb = EP1(nc.sbuf_tensor("stage_sb", [128, T], BF16))
            diag_bf = EP1(nc.sbuf_tensor("diag_bf", [128, KCONV * 128], BF16))
            cwcol = EP1(nc.sbuf_tensor("cwcol", [128, 34 * KCONV], F32))
            for k4_ in range(KCONV):
                kb.emit("sp", lambda h, k4_=k4_: h.dma_start(
                        cwcol[:, k4_ * 34:(k4_ + 1) * 34],
                        bass.AP(convw.tensor, k4_ * 4352, [[1, 128], [128, 34]]),
                        allow_slow_non_contiguous=True), (), ("cwcol",), 16, f"cw{k4_}")

            def load_wblk(ct, buf_i, cw):
                c0 = ct * 128
                dst = bass.AP(wblk[buf_i], 0, [[NKT * 128, 128], [cw, NKT], [1, cw]])
                src = bass.AP(win.tensor, c0, [[OUT_IN, 128], [128 * OUT_IN, NKT], [1, cw]])
                dma_pool(dst, src, (), (f"wblk{buf_i}",), name=f"wld{ct}")

            with nc.psum_tensor("psW", [128, T], F32) as psW, \
                 nc.psum_tensor("psC", [128, 1024], F32) as psC, \
                 nc.psum_tensor("psT", [128, 1024], BF16) as psT:

                def win_matmuls(ct, buf_i, cw):
                    fns = []
                    for k in range(NKT):
                        for nb in range(4):
                            def mm(h, k=k, nb=nb, buf_i=buf_i, cw=cw):
                                return h.matmul(psW[0:cw, nb * 512:(nb + 1) * 512],
                                                wblk[buf_i][:, k * cw:(k + 1) * cw],
                                                xlnT[:, k * TH + nb * 512: k * TH + (nb + 1) * 512],
                                                start=(k == 0), stop=(k == NKT - 1))
                            fns.append(mm)
                        def mmt(h, k=k, buf_i=buf_i, cw=cw):
                            return h.matmul(psC[0:cw, 0:3],
                                            wblk[buf_i][:, k * cw:(k + 1) * cw],
                                            xlnT[:, k * TH + 2048: k * TH + 2051],
                                            start=(k == 0), stop=(k == NKT - 1))
                        fns.append(mmt)
                    kb.group("pe", fns, (f"wblk{buf_i}", "xlnT"), ("psW", "psC"),
                             name=f"winmm{ct}")

                load_wblk(DTT, 0, 64)
                win_matmuls(DTT, 0, 64)
                dve(lambda h: h.tensor_copy(dt_raw[:, 0:2045], psW[0:H, 3:2048]),
                    ("psW",), ("dt_raw",), name="dtev")
                dve(lambda h: h.tensor_copy(dt_raw[:, 2045:2048], psC[0:H, 0:3]),
                    ("psC",), ("dt_raw",), name="dtev3")

                conv_cols = [BT_, CT_] + list(range(XT0, 64))
                load_wblk(conv_cols[0], 1, 128)
                for idx, ct in enumerate(conv_cols):
                    buf_i = (idx + 1) % 2
                    win_matmuls(ct, buf_i, 128)
                    if idx + 1 < len(conv_cols):
                        load_wblk(conv_cols[idx + 1], (idx + 2) % 2, 128)
                    dve(lambda h: h.tensor_copy(raw_sb[:, 0:2048], psW[:, :]),
                        ("psW",), ("raw_sb",), name=f"rawev{ct}")
                    dve(lambda h: h.tensor_copy(raw_sb[:, 2048:2051], psC[:, 0:3]),
                        ("psC",), ("raw_sb",), name=f"rawev3{ct}")
                    if dv_zx is not None:
                        dve(lambda h: h.tensor_copy(scratch_f[:, 0:T], psW[:, :]),
                            ("psW",), ("scratch_f",))
                        dma_sp(dv_zx[ct, :, :], scratch_f[:, 0:T], ("scratch_f",), ())
                    cti = 32 if ct == BT_ else (33 if ct == CT_ else ct - XT0)
                    dst_sb = B_fm if ct == BT_ else (C_fm if ct == CT_ else xconv_sb)
                    dst_key = "B_fm" if ct == BT_ else ("C_fm" if ct == CT_ else "xconv_sb")
                    for k4 in range(KCONV):
                        j = k4 * 34 + cti
                        dve(lambda h, j=j, k4=k4: h.tensor_scalar(
                                diag_bf[:, k4 * 128:(k4 + 1) * 128], ident_f[:],
                                cwcol[:, j:j + 1], None, ALU.mult),
                            ("ident_f", "cwcol"), ("diag_bf",), name=f"diag{ct}_{k4}")
                    for hp2 in range(2):
                        fns = []
                        for k4 in range(KCONV):
                            for nb in range(2):
                                def mm(h, k4=k4, nb=nb, hp2=hp2):
                                    off = k4 + hp2 * 1024 + nb * 512
                                    return h.matmul(psC[:, nb * 512:(nb + 1) * 512],
                                                    diag_bf[:, k4 * 128:(k4 + 1) * 128],
                                                    raw_sb[:, off: off + 512],
                                                    start=(k4 == 0), stop=(k4 == KCONV - 1))
                                fns.append(mm)
                        kb.group("pe", fns, ("diag_bf", "raw_sb"), ("psC",),
                                 name=f"conv{ct}_{hp2}")
                        act(lambda h, dst_sb=dst_sb, cti=cti, hp2=hp2: h.activation(
                                dst_sb[:, hp2 * 1024:(hp2 + 1) * 1024], psC[:, :], AF.Silu,
                                bias=convb_sb[:, cti:cti + 1]),
                            ("psC", "convb_sb"), (dst_key,), name=f"silu{ct}_{hp2}")
                    if dv_conv is not None:
                        dve(lambda h, dst_sb=dst_sb: h.tensor_copy(scratch_f[:, 0:T],
                                                                   dst_sb[:, 0:T]),
                            (dst_key,), ("scratch_f",))
                        dma_sp(dv_conv[cti, :, :], scratch_f[:, 0:T], ("scratch_f",), ())
                    if ct < 64:
                        for cb2 in range(2):
                            fns = []
                            for i in range(8):
                                cb = cb2 * 8 + i
                                def f(h, cb=cb, i=i):
                                    return h.transpose(psT[:, i * 128:(i + 1) * 128],
                                                       xconv_sb[:, cb * 128:(cb + 1) * 128],
                                                       ident_bf[:])
                                fns.append(f)
                            kb.group("pe", fns, ("xconv_sb", "ident_bf"), ("psT",),
                                     name=f"xtr{ct}_{cb2}")
                            dve(lambda h, cb2=cb2: h.tensor_copy(
                                    stage_sb[:, cb2 * 1024:(cb2 + 1) * 1024], psT[:, :]),
                                ("psT",), ("stage_sb",), name=f"xtrev{ct}_{cb2}")
                        ctx_ = ct - XT0
                        dst = bass.AP(sp_xtm.tensor, ctx_ * 128,
                                      [[INTER, 128], [128 * INTER, 16], [1, 128]])
                        src = bass.AP(stage_sb, 0, [[T, 128], [128, 16], [1, 128]])
                        dma_sp(dst, src, ("stage_sb",), ("sp_xtm",), name=f"xtmw{ct}")
                for cb2 in range(2):
                    fns = []
                    for i in range(8):
                        cb = cb2 * 8 + i
                        def f(h, cb=cb, i=i):
                            return h.transpose(psT[:, i * 128:(i + 1) * 128],
                                               B_fm[:, cb * 128:(cb + 1) * 128],
                                               ident_bf[:])
                        fns.append(f)
                    kb.group("pe", fns, ("B_fm", "ident_bf"), ("psT",), name=f"btr{cb2}")
                    dve(lambda h, cb2=cb2: h.tensor_copy(B_tm[:, cb2 * 1024:(cb2 + 1) * 1024],
                                                         psT[:, :]),
                        ("psT",), ("B_tm",), name=f"btrev{cb2}")
            kb.barrier()
            stk_p1.close()

            stk_dt = ExitStack()
            EDT = stk_dt.enter_context
            tA = EDT(nc.sbuf_tensor("tA", [H, T], F32))
            tB = EDT(nc.sbuf_tensor("tB", [H, T], F32))
            tC = EDT(nc.sbuf_tensor("tC", [H, T], F32))
            segmask = EDT(nc.sbuf_tensor("segmask", [H, T], F32))
            dce_sb = EDT(nc.sbuf_tensor("dce_sb", [H, NCH], F32))
            pool(lambda h: h.iota(tA[:], [[0, NCH], [1, CH]], channel_multiplier=0,
                                  allow_small_or_imprecise_dtypes=True), (), ("tA",))
            dve(lambda h: h.tensor_scalar(segmask[:], tA[:], 0.5, None, ALU.is_ge),
                ("tA",), ("segmask",))
            act(lambda h: h.activation(tA[:], dt_raw[:], AF.Abs, bias=dtb_sb[:, 0:1]),
                ("dt_raw", "dtb_sb"), ("tA",))
            act(lambda h: h.activation(tB[:], tA[:], AF.Exp, scale=-1.0), ("tA",), ("tB",))
            act(lambda h: h.activation(tA[:], tB[:], AF.Ln, bias=1.0), ("tB",), ("tA",))
            act(lambda h: h.activation(tB[:], dt_raw[:], AF.Relu, bias=dtb_sb[:, 0:1]),
                ("dt_raw", "dtb_sb"), ("tB",))
            dve(lambda h: h.tensor_add(tC[:], tA[:], tB[:]), ("tA", "tB"), ("tC",))
            dve(lambda h: h.tensor_scalar(tC[:], tC[:], DT_MAX, None, ALU.min),
                ("tC",), ("tC",))
            dve(lambda h: h.tensor_copy(dt_bf[:], tC[:]), ("tC",), ("dt_bf",))
            if dv_dt is not None:
                dma_sp(dv_dt[0, :, :], tC[:], ("tC",), ())
            dve(lambda h: h.tensor_scalar(tA[:], tC[:], acol[:, 0:1], None, ALU.mult),
                ("tC", "acol"), ("tA",), name="dA")
            dve(lambda h: h.tensor_tensor_scan(tB[:], segmask[:], tA[:], zero_col[0:H, 0:1],
                                               ALU.mult, ALU.add),
                ("segmask", "tA", "zero_col"), ("tB",), name="Acs")
            if dv_dt is not None:
                dma_sp(dv_dt[1, :, :], tB[:], ("tB",), ())
            act(lambda h: h.activation(tA[:], tA[:], AF.Exp), ("tA",), ("tA",), name="gexp")
            dve(lambda h: h.tensor_tensor(tA[:], tA[:], segmask[:], ALU.mult),
                ("tA", "segmask"), ("tA",), name="gmask")
            dma_sp(bass.AP(gflat_d.tensor, 0, [[CH, H], [H * CH, NCH], [1, CH]]), tA[:],
                   ("tA",), ("gflat_d",), name="gfl")
            act(lambda h: h.activation(sdec_bf[:], tB[:], AF.Exp), ("tB",), ("sdec_bf",))
            act(lambda h: h.activation(dce_sb[:], bass.AP(tB, 127, [[T, H], [CH, NCH]]),
                                       AF.Exp), ("tB",), ("dce_sb",))
            with nc.sbuf_tensor("dsf", [H, INTER], F32) as dsf, \
                 nc.sbuf_tensor("dsp", [H, INTER], F32) as dsp, \
                 nc.psum_tensor("ps_dce", [16, INTER], F32) as ps_dce:
                pool(lambda h: h.iota(dsf[:], [[1, H], [0, P]], channel_multiplier=0,
                                      allow_small_or_imprecise_dtypes=True), (), ("dsf",))
                pool(lambda h: h.iota(dsp[:], [[0, H], [0, P]], channel_multiplier=1,
                                      allow_small_or_imprecise_dtypes=True), (), ("dsp",))
                dve(lambda h: h.tensor_tensor(dsf[:], dsf[:], dsp[:], ALU.is_equal),
                    ("dsf", "dsp"), ("dsf",))
                fns = []
                for i in range(8):
                    def mm(h, i=i):
                        return h.matmul(ps_dce[:, i * 512:(i + 1) * 512],
                                        dce_sb[:], dsf[:, i * 512:(i + 1) * 512],
                                        start=True, stop=True)
                    fns.append(mm)
                kb.group("pe", fns, ("dce_sb", "dsf"), ("ps_dce",), name="dcemm")
                dve(lambda h: h.tensor_copy(dsp[0:16, :], ps_dce[:, :]),
                    ("ps_dce",), ("dsp",), name="dceev")
                dma_sp(dce_exp[:, :], dsp[0:16, :], ("dsp",), ("dce_exp",), name="dcew")
                kb.barrier()
            for c in range(NCH):
                act(lambda h, c=c: h.activation(tA[:, c * CH:(c + 1) * CH],
                                                tB[:, c * CH:(c + 1) * CH], AF.Exp,
                                                bias=tB[:, c * CH + 127: c * CH + 128],
                                                scale=-1.0),
                    ("tB",), ("tA",), name=f"decay{c}")
            dve(lambda h: h.tensor_tensor(dtdec_bf[:], tC[:], tA[:], ALU.mult),
                ("tC", "tA"), ("dtdec_bf",))
            if dv_dt is not None:
                dve(lambda h: h.tensor_copy(tB[:], dtdec_bf[:]), ("dtdec_bf",), ("tB",))
                dma_sp(dv_dt[2, :, :], tB[:], ("tB",), ())
                dve(lambda h: h.tensor_copy(tB[:], sdec_bf[:]), ("sdec_bf",), ("tB",))
                dma_sp(dv_dt[3, :, :], tB[:], ("tB",), ())
            kb.barrier()
            stk_dt.close()
            if stage < 2:
                _fin(nc, kb, ctx)
                return nc, dev_outs

            stk_s1 = ExitStack()
            ES1 = stk_s1.enter_context
            S_sb = ES1(nc.sbuf_tensor("S_sb", [128, INTER], F32))
            S_bf = ES1(nc.sbuf_tensor("S_bf", [128, INTER], BF16))
            xdtd = ES1(nc.sbuf_tensor("xdtd", [128, INTER], BF16))
            tS = ES1(nc.sbuf_tensor("tS", [128, INTER], F32))
            dcb1 = ES1(nc.sbuf_tensor("dcb1", [128, 2048], F32))
            pool(lambda h: h.memset(S_sb[:], 0.0), (), ("S_sb",))
            with nc.psum_tensor("psA", [128, 2048], F32) as psA, \
                 nc.psum_tensor("psB", [128, 1024], F32) as psB:
                dma_sp(xtm_buf[0][:], sp_xtm[0, :, :], ("sp_xtm",), ("xtm0",), name="p1x0")
                for c in range(NCH):
                    bi = c % 2
                    xk = f"xtm{bi}"
                    if c + 1 < NCH:
                        dma_sp(xtm_buf[(c + 1) % 2][:], sp_xtm[c + 1, :, :],
                               ("sp_xtm",), (f"xtm{(c + 1) % 2}",), name=f"p1x{c + 1}")
                    dve(lambda h: h.tensor_copy(S_bf[:], S_sb[:]), ("S_sb",), ("S_bf",))
                    dma_sp(sp_S[c, :, :], S_bf[:], ("S_bf",), (f"sp_S{c}",), name=f"p1Sw{c}")
                    if dv_S is not None:
                        dve(lambda h: h.tensor_copy(tS[:], S_sb[:]), ("S_sb",), ("tS",))
                        dma_sp(dv_S[c, :, :], tS[:], ("tS",), ())

                    for hf in range(2):
                        fns = []
                        for i in range(4):
                            def mm(h, hf=hf, i=i, c=c):
                                return h.matmul(psA[:, i * 512:(i + 1) * 512],
                                                dtdec_bf[:, c * CH:(c + 1) * CH],
                                                dsel_bf[:, hf * 2048 + i * 512: hf * 2048 + (i + 1) * 512],
                                                start=True, stop=True)
                            fns.append(mm)
                        kb.group("pe", fns, ("dtdec_bf", "dsel_bf"), ("psA",),
                                 name=f"p1bc{c}_{hf}")
                        dve(lambda h, hf=hf, bi=bi: h.tensor_tensor(
                                xdtd[:, hf * 2048:(hf + 1) * 2048],
                                xtm_buf[bi][:, hf * 2048:(hf + 1) * 2048],
                                psA[:, :], ALU.mult),
                            (xk, "psA"), ("xdtd",), name=f"p1xd{c}_{hf}")
                        dma_sp(dcb1[:], bass.AP(dce_exp.tensor,
                                                c * INTER + hf * 2048,
                                                [[0, 128], [1, 2048]]),
                               ("dce_exp",), ("dcb1",), name=f"dcer{c}_{hf}")
                        pool(lambda h, hf=hf: h.tensor_tensor(
                                tS[:, hf * 2048:(hf + 1) * 2048],
                                S_sb[:, hf * 2048:(hf + 1) * 2048],
                                dcb1[:], ALU.mult),
                            ("S_sb", "dcb1"), ("tS",), name=f"p1u1_{c}_{hf}")
                        for sh in range(2):
                            q0 = hf * 2048 + sh * 1024
                            fns = []
                            for i in range(2):
                                def mm(h, q0=q0, i=i, c=c):
                                    return h.matmul(psB[:, i * 512:(i + 1) * 512],
                                                    B_tm[:, c * CH:(c + 1) * CH],
                                                    xdtd[:, q0 + i * 512: q0 + (i + 1) * 512],
                                                    start=True, stop=True)
                                fns.append(mm)
                            kb.group("pe", fns, ("B_tm", "xdtd"), ("psB",),
                                     name=f"p1st{c}_{hf}_{sh}")
                            dve(lambda h, q0=q0: h.tensor_tensor(S_sb[:, q0:q0 + 1024],
                                                                 tS[:, q0:q0 + 1024],
                                                                 psB[:, :], ALU.add),
                                ("tS", "psB"), ("S_sb",), name=f"p1u2_{c}_{q0}")
            kb.barrier()

            dve(lambda h: h.tensor_copy(tS[:], S_sb[:]), ("S_sb",), ("tS",))
            dma_sp(cc_in.ap()[:, :], tS[:], ("tS",), ("cc_in",))
            kb.barrier()
            stk_s1.close()
            kb.emit("pool", lambda h: h.collective_compute(
                        "AllGather", ALU.bypass, [[0, 1], [2, 3], [4, 5], [6, 7]],
                        [cc_in.ap().opt()], [cc_out.ap().opt()]),
                    ("cc_in",), ("cc_out",), 1, name="ccag")
            if stage < 3:
                _fin(nc, kb, ctx)
                return nc, dev_outs

            stk_p2 = ExitStack()
            EP2 = stk_p2.enter_context
            wblk = [EP2(nc.sbuf_tensor(f"wblkz{i}", [128, NKT * 128], BF16)) for i in range(2)]
            raw_sb = EP2(nc.sbuf_tensor("raw_sbz", [128, TH], BF16))

            def load_wblk2(ct, buf_i, cw):
                c0 = ct * 128
                dst = bass.AP(wblk[buf_i], 0, [[NKT * 128, 128], [cw, NKT], [1, cw]])
                src = bass.AP(win.tensor, c0, [[OUT_IN, 128], [128 * OUT_IN, NKT], [1, cw]])
                dma_pool(dst, src, (), (f"wblk{buf_i}",), name=f"wld{ct}")

            with nc.psum_tensor("psW2", [128, T], F32) as psW2:
                load_wblk2(0, 0, 128)
                for ct in range(32):
                    buf_i = ct % 2
                    if ct + 1 < 32:
                        load_wblk2(ct + 1, (ct + 1) % 2, 128)
                    fns = []
                    for k in range(NKT):
                        for nb in range(4):
                            def mm(h, k=k, nb=nb, buf_i=buf_i):
                                return h.matmul(psW2[:, nb * 512:(nb + 1) * 512],
                                                wblk[buf_i][:, k * 128:(k + 1) * 128],
                                                xlnT[:, k * TH + 3 + nb * 512: k * TH + 3 + (nb + 1) * 512],
                                                start=(k == 0), stop=(k == NKT - 1))
                            fns.append(mm)
                    kb.group("pe", fns, (f"wblk{buf_i}", "xlnT"), ("psW2",), name=f"zmm{ct}")
                    dve(lambda h: h.tensor_copy(raw_sb[:, 0:T], psW2[:, :]),
                        ("psW2",), ("raw_sb",), name=f"zev{ct}")
                    dma_sp(sp_z[ct, :, :], raw_sb[:, 0:T], ("raw_sb",), ("sp_z",),
                           name=f"zw{ct}")
                    if dv_zx is not None:
                        dve(lambda h: h.tensor_copy(scratch_f[:, 0:T], psW2[:, :]),
                            ("psW2",), ("scratch_f",))
                        dma_sp(dv_zx[ct, :, :], scratch_f[:, 0:T], ("scratch_f",), ())
            kb.barrier()
            stk_p2.close()

        if stage < 4:
            _fin(nc, kb, ctx)
            return nc, dev_outs

        with ExitStack() as sc2:
            E2 = sc2.enter_context
            z_fm = [E2(nc.sbuf_tensor(f"zfm{i}", [128, INTER], BF16)) for i in range(2)]
            Ssp_sb = E2(nc.sbuf_tensor("Ssp_sb", [128, INTER], BF16))
            zs_bf = E2(nc.sbuf_tensor("zs_bf", [128, INTER], BF16))
            y_sb = E2(nc.sbuf_tensor("y_sb", [128, INTER], F32))
            yf_bf = E2(nc.sbuf_tensor("yf_bf", [128, INTER], BF16))
            L_sb = E2(nc.sbuf_tensor("L_sb", [128, 2048], F32))
            M_bf = E2(nc.sbuf_tensor("M_bf", [128, 2048], BF16))
            xdt2 = E2(nc.sbuf_tensor("xdt2", [128, INTER], BF16))
            xD2 = E2(nc.sbuf_tensor("xD2", [128, INTER], BF16))
            sd_sb = E2(nc.sbuf_tensor("sd_sb", [128, 1024], BF16))
            scm_bf = E2(nc.sbuf_tensor("scm_bf", [128, 128], BF16))
            gbc_sb = E2(nc.sbuf_tensor("gbc_sb", [128, 2048], F32))
            r_row = E2(nc.sbuf_tensor("r_row", [1, 128], F32))
            Scar = E2(nc.sbuf_tensor("Scar", [128, INTER], F32))
            Scar_bf = E2(nc.sbuf_tensor("Scar_bf", [128, INTER], BF16))
            Dbx_bf = E2(nc.sbuf_tensor("Dbx_bf", [128, INTER], BF16))
            identrep = E2(nc.sbuf_tensor("identrep", [128, 2048], F32))

            with nc.psum_tensor("ps_a", [128, 2048], F32) as ps_a, \
                 nc.psum_tensor("ps_y", [128, 1024], F32) as ps_y, \
                 nc.psum_tensor("ps_yo", [128, 1024], F32) as ps_yo:

                dve(lambda h: h.tensor_copy(identrep[:],
                                            bass.AP(ident_f, 0, [[128, 128], [0, 16], [1, 128]])),
                    ("ident_f",), ("identrep",), name="identrep")
                with nc.sbuf_tensor("drep", [H, 128], BF16) as drep, \
                     nc.sbuf_tensor("dcol", [H, 1], F32) as dcol:
                    dma_sp(dcol[:, 0:1], bass.AP(dvec.tensor, 0, [[1, H], [1, 1]]),
                           (), ("dcol",), name="dcol")
                    pool(lambda h: h.memset(drep[:], 1.0), (), ("drep",))
                    dve(lambda h: h.tensor_scalar(drep[:], drep[:], dcol[:, 0:1], None,
                                                  ALU.mult),
                        ("drep", "dcol"), ("drep",), name="drepsc")
                    for hf in range(2):
                        fns = []
                        for i in range(4):
                            def mm(h, hf=hf, i=i):
                                return h.matmul(ps_a[:, i * 512:(i + 1) * 512], drep[:],
                                                dsel_bf[:, hf * 2048 + i * 512: hf * 2048 + (i + 1) * 512],
                                                start=True, stop=True)
                            fns.append(mm)
                        kb.group("pe", fns, ("drep", "dsel_bf"), ("ps_a",), name=f"dbx{hf}")
                        dve(lambda h, hf=hf: h.tensor_copy(
                                Dbx_bf[:, hf * 2048:(hf + 1) * 2048], ps_a[:, :]),
                            ("ps_a",), ("Dbx_bf",), name=f"dbxev{hf}")
                    kb.barrier()
                dma_sp(y_sb[:], cc_out.ap()[0:128, :], ("cc_out",), ("y_sb",), name="ccr")
                dve(lambda h: h.tensor_scalar(Scar[:], y_sb[:], smask_sb[:, 0:1], None,
                                              ALU.mult),
                    ("y_sb", "smask_sb"), ("Scar",), name="scar_init")
                dve(lambda h: h.tensor_copy(Scar_bf[:], Scar[:]), ("Scar",), ("Scar_bf",))

                dma_sp(xtm_buf[0][:], sp_xtm[0, :, :], ("sp_xtm",), ("xtm0",), name="p2x0")
                dma_sp(bass.AP(z_fm[0], 0, [[INTER, 128], [128, 32], [1, 128]]),
                       bass.AP(sp_z.tensor, 0, [[T, 128], [128 * T, 32], [1, 128]]),
                       ("sp_z",), ("zfm0",), name="p2z0")

                for c in range(NCH):
                    bi = c % 2
                    xk, zk, sk = f"xtm{bi}", f"zfm{bi}", "Ssp"
                    dma_sp(Ssp_sb[:], sp_S[c, :, :], (f"sp_S{c}",), ("Ssp",),
                           name=f"p2S{c}")
                    if c + 1 < NCH:
                        nb_ = (c + 1) % 2
                        dma_sp(xtm_buf[nb_][:], sp_xtm[c + 1, :, :],
                               ("sp_xtm",), (f"xtm{nb_}",), name=f"p2x{c + 1}")
                        dma_sp(bass.AP(z_fm[nb_], 0, [[INTER, 128], [128, 32], [1, 128]]),
                               bass.AP(sp_z.tensor, (c + 1) * CH,
                                       [[T, 128], [128 * T, 32], [1, 128]]),
                               ("sp_z",), (f"zfm{nb_}",), name=f"p2z{c + 1}")
                    for hf in range(2):
                        fns = []
                        for i in range(4):
                            def mm(h, hf=hf, i=i, c=c):
                                return h.matmul(ps_a[:, i * 512:(i + 1) * 512],
                                                dt_bf[:, c * CH:(c + 1) * CH],
                                                dsel_bf[:, hf * 2048 + i * 512: hf * 2048 + (i + 1) * 512],
                                                start=True, stop=True)
                            fns.append(mm)
                        kb.group("pe", fns, ("dt_bf", "dsel_bf"), ("ps_a",),
                                 name=f"dtbc{c}_{hf}")
                        dve(lambda h, hf=hf, bi=bi: h.tensor_tensor(
                                xdt2[:, hf * 2048:(hf + 1) * 2048],
                                xtm_buf[bi][:, hf * 2048:(hf + 1) * 2048],
                                ps_a[:, :], ALU.mult),
                            (xk, "ps_a"), ("xdt2",), name=f"xdt{c}_{hf}")
                    pool(lambda h, bi=bi: h.tensor_tensor(xD2[:], xtm_buf[bi][:],
                                                          Dbx_bf[:], ALU.mult),
                        (xk, "Dbx_bf"), ("xD2",), name=f"xD{c}")
                    pe1(lambda h, c=c: h.matmul(ps_a[:, 0:128], B_fm[:, c * CH:(c + 1) * CH],
                                                C_fm[:, c * CH:(c + 1) * CH],
                                                start=True, stop=True),
                        ("B_fm", "C_fm"), ("ps_a",), name=f"sc{c}")
                    dve(lambda h: h.tensor_tensor(scm_bf[:], ps_a[:, 0:128], triu_f[:],
                                                  ALU.mult),
                        ("ps_a", "triu_f"), ("scm_bf",), name=f"scm{c}")
                    act(lambda h, bi=bi: h.activation(zs_bf[:], z_fm[bi][:], AF.Silu),
                        (zk,), ("zs_bf",), name=f"siluz{c}")
                    for g4 in range(4):
                        dma_sp(gbc_sb[:], bass.AP(gflat_d.tensor,
                                                  c * H * CH + g4 * 2048,
                                                  [[0, 128], [1, 2048]]),
                               ("gflat_d",), ("gbc_sb",), name=f"p2gf{c}_{g4}")
                        dve(lambda h: h.tensor_tensor_scan(
                                L_sb[:], gbc_sb[:], identrep[:],
                                zero_col[:, 0:1], ALU.mult, ALU.add),
                            ("gbc_sb", "identrep", "zero_col"), ("L_sb",),
                            name=f"Lscan{c}_{g4}")
                        dve(lambda h: h.tensor_tensor(
                                M_bf[:], L_sb[:],
                                bass.AP(scm_bf, 0, [[128, 128], [0, 16], [1, 128]]),
                                ALU.mult),
                            ("L_sb", "scm_bf"), ("M_bf",), name=f"Mc{c}_{g4}")
                        fns = []
                        for h16 in range(16):
                            def mm(h, h16=h16, g4=g4):
                                hh = g4 * 16 + h16
                                return h.matmul(ps_y[:, h16 * 64:(h16 + 1) * 64],
                                                M_bf[:, h16 * 128:(h16 + 1) * 128],
                                                xdt2[:, hh * 64:(hh + 1) * 64],
                                                start=(h16 % 8 == 0), stop=False)
                            fns.append(mm)
                        for i in range(2):
                            def mm(h, i=i, g4=g4):
                                return h.matmul(ps_y[:, i * 512:(i + 1) * 512],
                                                ident_bf[:],
                                                xD2[:, g4 * 1024 + i * 512: g4 * 1024 + (i + 1) * 512],
                                                start=False, stop=True)
                            fns.append(mm)
                        kb.group("pe", fns, ("M_bf", "xdt2", "ident_bf", "xD2"), ("ps_y",),
                                 name=f"ydiag{c}_{g4}")
                        fns = []
                        for i in range(2):
                            def mm(h, i=i, g4=g4, c=c):
                                return h.matmul(ps_yo[:, i * 512:(i + 1) * 512],
                                                C_fm[:, c * CH:(c + 1) * CH],
                                                Ssp_sb[:, g4 * 1024 + i * 512: g4 * 1024 + (i + 1) * 512],
                                                start=True, stop=False)
                            fns.append(mm)
                        for i in range(2):
                            def mm(h, i=i, g4=g4, c=c):
                                return h.matmul(ps_yo[:, i * 512:(i + 1) * 512],
                                                C_fm[:, c * CH:(c + 1) * CH],
                                                Scar_bf[:, g4 * 1024 + i * 512: g4 * 1024 + (i + 1) * 512],
                                                start=False, stop=True)
                            fns.append(mm)
                        kb.group("pe", fns, ("C_fm", sk, "Scar_bf"), ("ps_yo",),
                                 name=f"yoff{c}_{g4}")
                        fns = []
                        for i in range(2):
                            def mm(h, g4=g4, i=i, c=c):
                                return h.matmul(ps_a[:, i * 512:(i + 1) * 512],
                                                sdec_bf[:, c * CH:(c + 1) * CH],
                                                dsel_bf[:, g4 * 1024 + i * 512: g4 * 1024 + (i + 1) * 512],
                                                start=True, stop=True)
                            fns.append(mm)
                        kb.group("pe", fns, ("sdec_bf", "dsel_bf"), ("ps_a",),
                                 name=f"sdbc{c}_{g4}")
                        dve(lambda h: h.tensor_copy(sd_sb[:], ps_a[:, 0:1024]),
                            ("ps_a",), ("sd_sb",), name=f"sdev{c}_{g4}")
                        dve(lambda h: h.tensor_tensor(L_sb[:, 0:1024], sd_sb[:],
                                                      ps_yo[:, :], ALU.mult),
                            ("sd_sb", "ps_yo"), ("L_sb",), name=f"yo1{c}_{g4}")
                        dve(lambda h, g4=g4: h.tensor_tensor(
                                y_sb[:, g4 * 1024:(g4 + 1) * 1024],
                                L_sb[:, 0:1024], ps_y[:, :], ALU.add),
                            ("L_sb", "ps_y"), ("y_sb",), name=f"yo2{c}_{g4}")
                    if dv_y is not None:
                        dma_sp(dv_y[c, :, :], y_sb[:], ("y_sb",), ())
                    for q in range(4):
                        fns = []
                        for i in range(8):
                            kt = q * 8 + i
                            def f(h, kt=kt, i=i):
                                return h.transpose(ps_y[:, i * 128:(i + 1) * 128],
                                                   y_sb[:, kt * 128:(kt + 1) * 128],
                                                   ident_f[:])
                            fns.append(f)
                        kb.group("pe", fns, ("y_sb", "ident_f"), ("ps_y",),
                                 name=f"ytr{c}_{q}")
                        dve(lambda h, q=q: h.tensor_tensor(yf_bf[:, q * 1024:(q + 1) * 1024],
                                                           ps_y[:, :],
                                                           zs_bf[:, q * 1024:(q + 1) * 1024],
                                                           ALU.mult),
                            ("ps_y", "zs_bf"), ("yf_bf",), name=f"gate{c}_{q}")
                    pool(lambda h: h.tensor_tensor(xdt2[:], yf_bf[:], yf_bf[:], ALU.mult),
                        ("yf_bf",), ("xdt2",), name=f"sq{c}")
                    fns = []
                    for kt in range(32):
                        def mm(h, kt=kt):
                            return h.matmul(ps_yo[0:1, 0:128], ones128_bf[:, 0:1],
                                            xdt2[:, kt * 128:(kt + 1) * 128],
                                            start=(kt == 0), stop=(kt == 31))
                        fns.append(mm)
                    kb.group("pe", fns, ("xdt2", "ones128_bf"), ("ps_yo",),
                             name=f"var{c}")
                    dve(lambda h: h.tensor_scalar(r_row[:], ps_yo[0:1, 0:128], 1.0 / INTER,
                                                  EPS_RMS, ALU.mult, ALU.add),
                        ("ps_yo",), ("r_row",), name=f"v1{c}")
                    act(lambda h: h.activation(r_row[:], r_row[:], AF.Sqrt),
                        ("r_row",), ("r_row",), name=f"v2{c}")
                    dve(lambda h: h.reciprocal(r_row[:], r_row[:]), ("r_row",), ("r_row",),
                        name=f"v3{c}")
                    dma_sp(bass.AP(r_d.tensor, c * CH, [[1, 1], [1, CH]]), r_row[:],
                           ("r_row",), ("r_d",), name=f"rw{c}")
                    if dv_rd is not None:
                        dma_sp(dv_rd[c:c + 1, :], r_row[:], ("r_row",), ())
                    if dv_yf is not None:
                        dma_pool(bass.AP(dv_yf.tensor, c * 32 * 128 * CH,
                                         [[CH, 128], [128 * CH, 32], [1, CH]]),
                                 bass.AP(yf_bf, 0, [[INTER, 128], [CH, 32], [1, CH]]),
                                 ("yf_bf",), ())
                    dma_sp(bass.AP(sp_yf.tensor, c * 32 * 128 * CH,
                                   [[CH, 128], [128 * CH, 32], [1, CH]]),
                           bass.AP(yf_bf, 0, [[INTER, 128], [CH, 32], [1, CH]]),
                           ("yf_bf",), (f"sp_yf{c}",), name=f"yfw{c}")
                    if c + 1 < NCH:
                        for hf in range(2):
                            dma_sp(gbc_sb[:], bass.AP(dce_exp.tensor,
                                                      c * INTER + hf * 2048,
                                                      [[0, 128], [1, 2048]]),
                                   ("dce_exp",), ("gbc_sb",), name=f"p2dcer{c}_{hf}")
                            pool(lambda h, hf=hf: h.tensor_tensor(
                                    y_sb[:, hf * 2048: hf * 2048 + 2048],
                                    Scar[:, hf * 2048: hf * 2048 + 2048],
                                    gbc_sb[:], ALU.mult),
                                ("Scar", "gbc_sb"), ("y_sb",),
                                name=f"scar{c}_{hf}")
                        pool(lambda h: h.tensor_copy(Scar[:], y_sb[:]),
                            ("y_sb",), ("Scar",), name=f"scarcp{c}")
                        pool(lambda h: h.tensor_copy(Scar_bf[:], Scar[:]),
                            ("Scar",), ("Scar_bf",), name=f"scarbf{c}")
            kb.barrier()

        stk_ab.close()
        if stage < 5:
            _fin(nc, kb, ctx)
            return nc, dev_outs

        x2_fm = E(nc.sbuf_tensor("x2_fm", [128, NKT * T], BF16))
        with ExitStack() as sc3:
            E3 = sc3.enter_context
            x2t = E3(nc.sbuf_tensor("x2t", [128, HID], BF16))
            wo_sb = E3(nc.sbuf_tensor("wo_sb", [128, 16 * HID], BF16))
            yft = [E3(nc.sbuf_tensor(f"yft{i}", [128, 16 * 128], BF16)) for i in range(2)]
            xres = [E3(nc.sbuf_tensor(f"xres{i}", [128, HID], F32)) for i in range(2)]
            part_b = [E3(nc.sbuf_tensor(f"part{i}", [128, HID], F32)) for i in range(2)]
            h1_sb = E3(nc.sbuf_tensor("h1_sb", [128, HID], F32))
            rcol = E3(nc.sbuf_tensor("rcol", [128, 1], F32))

            with nc.psum_tensor("ps_w", [128, HID], F32) as ps_w, \
                 nc.psum_tensor("ps_tx", [128, 1024], BF16) as ps_tx:
                for half in range(2):
                    for kt in range(16):
                        gk = half * 16 + kt
                        dma_sp(scratch_f[:], wout[gk * 128:(gk + 1) * 128, :],
                               (), ("scratch_f",), name=f"wold{gk}")
                        dve(lambda h, kt=kt, gk=gk: h.tensor_scalar(
                                wo_sb[:, kt * HID:(kt + 1) * HID],
                                scratch_f[:],
                                nw_sb[:, gk:gk + 1], None, ALU.mult),
                            ("scratch_f", "nw_sb"), (f"wo{kt}",), name=f"wosc{gk}")
                    if dv_wo is not None:
                        for kt in range(16):
                            dve(lambda h, kt=kt: h.tensor_copy(scratch_f[:],
                                    wo_sb[:, kt * HID:(kt + 1) * HID]),
                                (f"wo{kt}",), ("scratch_f",))
                            dma_sp(dv_wo[half, kt, :, :], scratch_f[:], ("scratch_f",), ())
                    if dv_yfsp is not None and half == 0:
                        for c_ in range(NCH):
                            kb.emit("pool", lambda h, c_=c_: h.dma_start(
                                    dv_yfsp[c_, :, :, :], sp_yf[c_, :, :, :]),
                                    (f"sp_yf{c_}",), (), 16, f"yfspd{c_}")
                    for tt in range(NCH):
                        pb = tt % 2
                        dma_sp(bass.AP(yft[pb], 0, [[16 * 128, 128], [128, 16], [1, 128]]),
                               bass.AP(sp_yf.tensor,
                                       tt * 32 * 128 * CH + half * 16 * 128 * CH,
                                       [[CH, 128], [128 * CH, 16], [1, CH]]),
                               (f"sp_yf{tt}",), (f"yft{pb}",), name=f"yfr{half}_{tt}")
                        fns = []
                        for kt in range(16):
                            for nb in range(4):
                                def mm(h, kt=kt, nb=nb, pb=pb):
                                    return h.matmul(ps_w[:, nb * 512:(nb + 1) * 512],
                                                    yft[pb][:, kt * 128:(kt + 1) * 128],
                                                    wo_sb[:, kt * HID + nb * 512: kt * HID + (nb + 1) * 512],
                                                    start=(kt == 0), stop=(kt == 15))
                                fns.append(mm)
                        kb.group("pe", fns,
                                 tuple([f"yft{pb}"] + [f"wo{k}" for k in range(16)]),
                                 ("ps_w",), name=f"womm{half}_{tt}")
                        if half == 0:
                            dve(lambda h, pb=pb: h.tensor_copy(part_b[pb][:], ps_w[:, :]),
                                ("ps_w",), (f"partb{pb}",), name=f"pev{tt}")
                            dma_sp(sp_part[tt, :, :], part_b[pb][:], (f"partb{pb}",),
                                   (f"sp_part{tt}",), name=f"pw{tt}")
                        else:
                            dma_sp(part_b[pb][:], sp_part[tt, :, :], (f"sp_part{tt}",),
                                   (f"partb{pb}",), name=f"pr{tt}")
                            dma_sp(xres[pb][:], xin[3 + tt * 128: 3 + (tt + 1) * 128, :],
                                   (), (f"xres{pb}",), name=f"xrr{tt}")
                            dma_sp(rcol[:], bass.AP(r_d.tensor, tt * CH, [[1, 128], [1, 1]]),
                                   ("r_d",), ("rcol",), name=f"rcl{tt}")
                            dve(lambda h, pb=pb: h.tensor_tensor(h1_sb[:], part_b[pb][:],
                                                                 ps_w[:, :], ALU.add),
                                (f"partb{pb}", "ps_w"), ("h1_sb",), name=f"h1a{tt}")
                            dve(lambda h, pb=pb: h.scalar_tensor_tensor(
                                    h1_sb[:], h1_sb[:], rcol[:, 0:1], xres[pb][:],
                                    ALU.mult, ALU.add),
                                ("h1_sb", "rcol", f"xres{pb}"), ("h1_sb",), name=f"h1b{tt}")
                            dma_sp(sp_h1[tt, :, :], h1_sb[:], ("h1_sb",), (f"sp_h1{tt}",),
                                   name=f"h1w{tt}")
                            if dv_h1 is not None:
                                dma_sp(dv_h1[tt, :, :], h1_sb[:], ("h1_sb",), ())
                            ln_stats_apply(h1_sb, "h1_sb", x2t[:], "x2t")
                            if dv_x2 is not None:
                                dve(lambda h: h.tensor_copy(scratch_f[:], x2t[:]),
                                    ("x2t",), ("scratch_f",))
                                dma_sp(dv_x2[tt, :, :], scratch_f[:], ("scratch_f",), ())
                            for kg in range(2):
                                fns = []
                                for i in range(8):
                                    kt = kg * 8 + i
                                    def f(h, kt=kt, i=i):
                                        return h.transpose(ps_tx[:, i * 128:(i + 1) * 128],
                                                           x2t[:, kt * 128:(kt + 1) * 128],
                                                           ident_bf[:])
                                    fns.append(f)
                                kb.group("pe", fns, ("x2t", "ident_bf"), ("ps_tx",),
                                         name=f"x2tr{tt}_{kg}")
                                def ev(h, tt=tt, kg=kg):
                                    return h.tensor_copy(
                                        bass.AP(x2_fm, kg * 8 * T + tt * 128,
                                                [[NKT * T, 128], [T, 8], [1, 128]]),
                                        bass.AP(ps_tx, 0, [[1024, 128], [128, 8], [1, 128]]))
                                dve(ev, ("ps_tx",), ("x2_fm",), name=f"x2ev{tt}_{kg}")
            kb.barrier()

        if stage < 6:
            _fin(nc, kb, ctx)
            return nc, dev_outs

        with ExitStack() as sc4:
            E4 = sc4.enter_context
            stk_rt = ExitStack()
            ER = stk_rt.enter_context
            rw_sb = ER(nc.sbuf_tensor("rw_sb", [8, HID], F32))
            rwT_bf = ER(nc.sbuf_tensor("rwT_bf", [128, NKT * 8], BF16))
            sc_tm = ER(nc.sbuf_tensor("sc_tm", [128, NKT * 8], F32))
            w_tm = ER(nc.sbuf_tensor("w_tm", [128, NKT * 8], F32))
            w_fm = ER(nc.sbuf_tensor("w_fm", [8, T], F32))
            t8a = ER(nc.sbuf_tensor("t8a", [128, 8], F32))
            t8b = ER(nc.sbuf_tensor("t8b", [128, 8], F32))
            mx1 = ER(nc.sbuf_tensor("mx1", [128, 1], F32))
            mx2 = ER(nc.sbuf_tensor("mx2", [128, 1], F32))
            p1c = ER(nc.sbuf_tensor("p1c", [128, 1], F32))
            p2c = ER(nc.sbuf_tensor("p2c", [128, 1], F32))
            dma_sp(rw_sb[:], rwt[:, :], (), ("rw_sb",))
            with nc.psum_tensor("ps_r", [128, 512], F32) as ps_r, \
                 nc.psum_tensor("ps_s", [8, T], F32) as ps_s:
                for kt in range(NKT):
                    pe1(lambda h, kt=kt: h.transpose(ps_r[:, 0:8],
                                                     rw_sb[0:8, kt * 128:(kt + 1) * 128],
                                                     ident_f[0:8, 0:8]),
                        ("rw_sb", "ident_f"), ("ps_r",), name=f"rwtr{kt}")
                    dve(lambda h, kt=kt: h.tensor_copy(rwT_bf[:, kt * 8:(kt + 1) * 8],
                                                       ps_r[:, 0:8]),
                        ("ps_r",), ("rwT_bf",), name=f"rwev{kt}")
                fns = []
                for kt in range(NKT):
                    for nb in range(4):
                        def mm(h, kt=kt, nb=nb):
                            return h.matmul(ps_s[0:8, nb * 512:(nb + 1) * 512],
                                            rwT_bf[:, kt * 8:(kt + 1) * 8],
                                            x2_fm[:, kt * T + nb * 512: kt * T + (nb + 1) * 512],
                                            start=(kt == 0), stop=(kt == NKT - 1))
                        fns.append(mm)
                kb.group("pe", fns, ("rwT_bf", "x2_fm"), ("ps_s",), name="scmm")
                dve(lambda h: h.tensor_copy(w_fm[:], ps_s[:, :]), ("ps_s",), ("w_fm",),
                    name="scev")
                for tt in range(NCH):
                    pe1(lambda h, tt=tt: h.transpose(ps_r[:, 0:8],
                                                     w_fm[0:8, tt * 128:(tt + 1) * 128],
                                                     ident_f[0:8, 0:8]),
                        ("w_fm", "ident_f"), ("ps_r",), name=f"sctr{tt}")
                    dve(lambda h, tt=tt: h.tensor_copy(sc_tm[:, tt * 8:(tt + 1) * 8],
                                                       ps_r[:, 0:8]),
                        ("ps_r",), ("sc_tm",), name=f"scev{tt}")
                    if dv_sc is not None:
                        dve(lambda h, tt=tt: h.tensor_copy(scratch_f[:, 0:8],
                                                           sc_tm[:, tt * 8:(tt + 1) * 8]),
                            ("sc_tm",), ("scratch_f",))
                        dma_sp(dv_sc[tt, :, :], scratch_f[:, 0:8], ("scratch_f",), ())
                for tt in range(NCH):
                    s8 = sc_tm[:, tt * 8:(tt + 1) * 8]
                    dve(lambda h, s8=s8: h.reduce_max(mx1[:], s8, AX.X), ("sc_tm",), ("mx1",))
                    dve(lambda h, s8=s8: h.tensor_scalar(t8a[:], s8, mx1[:, 0:1], None,
                                                         ALU.is_ge),
                        ("sc_tm", "mx1"), ("t8a",))
                    dve(lambda h, s8=s8: h.scalar_tensor_tensor(t8b[:], t8a[:], -1e30,
                                                                s8, ALU.mult, ALU.add),
                        ("t8a", "sc_tm"), ("t8b",))
                    dve(lambda h: h.reduce_max(mx2[:], t8b[:], AX.X), ("t8b",), ("mx2",))
                    dve(lambda h, s8=s8: h.tensor_scalar(t8b[:], s8, mx2[:, 0:1], None,
                                                         ALU.is_ge),
                        ("sc_tm", "mx2"), ("t8b",))
                    dve(lambda h: h.tensor_sub(v_c[:], mx1[:], mx2[:]), ("mx1", "mx2"),
                        ("v_c",))
                    act(lambda h: h.activation(p1c[:], v_c[:], AF.Sigmoid), ("v_c",), ("p1c",))
                    dve(lambda h: h.tensor_scalar(p2c[:], p1c[:], -1.0, 1.0, ALU.mult,
                                                  ALU.add), ("p1c",), ("p2c",))
                    dve(lambda h: h.tensor_sub(t8b[:], t8b[:], t8a[:]), ("t8b", "t8a"),
                        ("t8b",))
                    dve(lambda h: h.tensor_scalar(t8a[:], t8a[:], p1c[:, 0:1], None,
                                                  ALU.mult), ("t8a", "p1c"), ("t8a",))
                    dve(lambda h: h.tensor_scalar(t8b[:], t8b[:], p2c[:, 0:1], None,
                                                  ALU.mult), ("t8b", "p2c"), ("t8b",))
                    dve(lambda h, tt=tt: h.tensor_add(w_tm[:, tt * 8:(tt + 1) * 8],
                                                      t8a[:], t8b[:]),
                        ("t8a", "t8b"), ("w_tm",), name=f"wall{tt}")
                    if dv_wall is not None:
                        dve(lambda h, tt=tt: h.tensor_copy(scratch_f[:, 0:8],
                                                           w_tm[:, tt * 8:(tt + 1) * 8]),
                            ("w_tm",), ("scratch_f",))
                        dma_sp(dv_wall[tt, :, :], scratch_f[:, 0:8], ("scratch_f",), ())
                for tt in range(NCH):
                    pe1(lambda h, tt=tt: h.transpose(ps_r[0:8, 0:128],
                                                     w_tm[:, tt * 8:(tt + 1) * 8],
                                                     ident_f[:]),
                        ("w_tm", "ident_f"), ("ps_r",), name=f"wtr{tt}")
                    dve(lambda h, tt=tt: h.tensor_copy(w_fm[0:8, tt * 128:(tt + 1) * 128],
                                                       ps_r[0:8, 0:128]),
                        ("ps_r",), ("w_fm",), name=f"wfm{tt}")
                dma_sp(wrow_d[:, :], w_fm[:], ("w_fm",), ("wrow_d",), name="wrw")
            kb.barrier()
            stk_rt.close()

            for gi in range(2):
                stk_h = ExitStack()
                EH = stk_h.enter_context
                w1_sb = [EH(nc.sbuf_tensor(f"w1s{i}_{gi}", [128, NKT * 512], BF16)) for i in range(2)]
                w3_sb = [EH(nc.sbuf_tensor(f"w3s{i}_{gi}", [128, NKT * 512], BF16)) for i in range(2)]
                wb_sb = EH(nc.sbuf_tensor(f"wb_sb{gi}", [128, 512], F32))
                wr_sb = EH(nc.sbuf_tensor(f"wr_sb{gi}", [1, 512], F32))
                g_sb = EH(nc.sbuf_tensor(f"g_sb{gi}", [128, 2048], BF16))
                s1_f = EH(nc.sbuf_tensor(f"s1_f{gi}", [128, 512], F32))
                s2_f = EH(nc.sbuf_tensor(f"s2_f{gi}", [128, 512], F32))

                def load_w13(e, buf_i, which, wt):
                    tgt = w1_sb[buf_i] if which == 1 else w3_sb[buf_i]
                    dst = bass.AP(tgt, 0, [[NKT * 512, 128], [512, NKT], [1, 512]])
                    src = bass.AP(wt.tensor, e * HID * 512,
                                  [[512, 128], [512 * 128, NKT], [1, 512]])
                    dma_pool(dst, src, (), (f"w{which}s{buf_i}",), name=f"w{which}ld{e}")

                with nc.psum_tensor(f"ps_h1_{gi}", [128, 2048], F32) as ps_h1, \
                     nc.psum_tensor(f"ps_h3_{gi}", [128, 2048], F32) as ps_h3:
                    load_w13(gi * 4, 0, 1, w1)
                    load_w13(gi * 4, 0, 3, w3)
                    for ei in range(4):
                        e = gi * 4 + ei
                        bi = ei % 2
                        if ei + 1 < 4:
                            load_w13(e + 1, (ei + 1) % 2, 1, w1)
                            load_w13(e + 1, (ei + 1) % 2, 3, w3)
                        for tb in range(4):
                            dma_sp(wb_sb[:], bass.AP(wrow_d.tensor, e * T + tb * 512,
                                                     [[0, 128], [1, 512]]),
                                   ("wrow_d",), ("wb_sb",), name=f"wrr{e}_{tb}")
                            for which, wsb, wkey in ((1, w1_sb, f"w1s{bi}"),
                                                     (3, w3_sb, f"w3s{bi}")):
                                psh = ps_h1 if which == 1 else ps_h3
                                phk = "ps_h1" if which == 1 else "ps_h3"
                                fns = []
                                for ft in range(4):
                                    for kt in range(NKT):
                                        def mm(h, kt=kt, ft=ft, tb=tb, bi=bi, psh=psh,
                                               wsb=wsb):
                                            return h.matmul(
                                                psh[:, ft * 512:(ft + 1) * 512],
                                                wsb[bi][:, kt * 512 + ft * 128: kt * 512 + (ft + 1) * 128],
                                                x2_fm[:, kt * T + tb * 512: kt * T + (tb + 1) * 512],
                                                start=(kt == 0), stop=(kt == NKT - 1))
                                        fns.append(mm)
                                kb.group("pe", fns, (wkey, "x2_fm"), (phk,),
                                         name=f"h{which}_{e}_{tb}")
                            for ft in range(4):
                                act(lambda h, ft=ft: h.activation(
                                        s1_f[:], ps_h1[:, ft * 512:(ft + 1) * 512], AF.Silu),
                                    ("ps_h1",), ("s1_f",), name=f"msl{e}_{tb}_{ft}")
                                dve(lambda h, ft=ft: h.tensor_tensor(
                                        s2_f[:], s1_f[:], ps_h3[:, ft * 512:(ft + 1) * 512],
                                        ALU.mult),
                                    ("s1_f", "ps_h3"), ("s2_f",), name=f"mg1{e}_{tb}_{ft}")
                                dve(lambda h, ft=ft: h.tensor_tensor(
                                        g_sb[:, ft * 512:(ft + 1) * 512], s2_f[:], wb_sb[:],
                                        ALU.mult),
                                    ("s2_f", "wb_sb"), ("g_sb",), name=f"mg2{e}_{tb}_{ft}")
                            dma_sp(bass.AP(sp_g.tensor, ei * 4 * 128 * T + tb * 512,
                                           [[T, 128], [128 * T, 4], [1, 512]]),
                                   bass.AP(g_sb, 0, [[2048, 128], [512, 4], [1, 512]]),
                                   ("g_sb",), (f"sp_g{ei}_{tb}",), name=f"gw{e}_{tb}")
                kb.barrier()
                stk_h.close()
                stk_w2 = ExitStack()
                EW2 = stk_w2.enter_context
                w2g_sb = EW2(nc.sbuf_tensor(f"w2g_sb{gi}", [128, 16 * HID], BF16))
                gt_buf = [EW2(nc.sbuf_tensor(f"gt{i}_{gi}", [128, 16 * 128], BF16)) for i in range(2)]
                moe_b = [EW2(nc.sbuf_tensor(f"moe{i}_{gi}", [128, HID], F32)) for i in range(2)]
                h1r = [EW2(nc.sbuf_tensor(f"h1r{i}_{gi}", [128, HID], F32)) for i in range(2)]
                with nc.psum_tensor(f"ps_o_{gi}", [128, HID], F32) as ps_o:
                    for ei in range(4):
                        e = gi * 4 + ei
                        for kt4 in range(4):
                            dma_pool(bass.AP(w2g_sb, (ei * 4 + kt4) * HID,
                                             [[16 * HID, 128], [1, HID]]),
                                     bass.AP(w2.tensor, e * 512 * HID + kt4 * 128 * HID,
                                             [[HID, 128], [1, HID]]),
                                     (), (f"w2g{ei}_{kt4}",), name=f"w2ld{e}_{kt4}")
                    for tt in range(NCH):
                        pb = tt % 2
                        dma_sp(bass.AP(gt_buf[pb], 0, [[16 * 128, 128], [128, 16], [1, 128]]),
                               bass.AP(sp_g.tensor, tt * 128,
                                       [[T, 128], [128 * T, 16], [1, 128]]),
                               tuple(f"sp_g{ei}_{tt // 4}" for ei in range(4)),
                               (f"gt{pb}",), name=f"gr{gi}_{tt}")
                        fns = []
                        for j in range(16):
                            for nb in range(4):
                                def mm(h, j=j, nb=nb, pb=pb):
                                    return h.matmul(
                                        ps_o[:, nb * 512:(nb + 1) * 512],
                                        gt_buf[pb][:, j * 128:(j + 1) * 128],
                                        w2g_sb[:, j * HID + nb * 512: j * HID + (nb + 1) * 512],
                                        start=(j == 0), stop=(j == 15))
                                fns.append(mm)
                        kb.group("pe", fns,
                                 tuple([f"gt{pb}"] +
                                       [f"w2g{ei}_{k4}" for ei in range(4) for k4 in range(4)]),
                                 ("ps_o",), name=f"w2mm{gi}_{tt}")
                        if gi == 0:
                            dve(lambda h, pb=pb: h.tensor_copy(moe_b[pb][:], ps_o[:, :]),
                                ("ps_o",), (f"moeb{pb}",), name=f"moev{tt}")
                            dma_sp(sp_part[tt, :, :], moe_b[pb][:], (f"moeb{pb}",),
                                   (f"sp_part{tt}",), name=f"mow{tt}")
                        else:
                            dma_sp(moe_b[pb][:], sp_part[tt, :, :], (f"sp_part{tt}",),
                                   (f"moeb{pb}",), name=f"mor{tt}")
                            dma_sp(h1r[pb][:], sp_h1[tt, :, :], (f"sp_h1{tt}",),
                                   (f"h1r{pb}",), name=f"h1rr{tt}")
                            dve(lambda h, pb=pb: h.tensor_tensor(moe_b[pb][:],
                                                                 moe_b[pb][:], ps_o[:, :],
                                                                 ALU.add),
                                (f"moeb{pb}", "ps_o"), (f"moeb{pb}",), name=f"oa{tt}")
                            dve(lambda h, pb=pb: h.tensor_tensor(moe_b[pb][:],
                                                                 moe_b[pb][:], h1r[pb][:],
                                                                 ALU.add),
                                (f"moeb{pb}", f"h1r{pb}"), (f"moeb{pb}",), name=f"ob{tt}")
                            dma_sp(out[tt * 128:(tt + 1) * 128, :], moe_b[pb][:],
                                   (f"moeb{pb}",), ("OUT",), name=f"outw{tt}")
                kb.barrier()
                stk_w2.close()

        _fin(nc, kb, ctx)
    return nc, dev_outs


def _fin(nc, kb, ctx):
    kb.barrier()
    sems = {e: ctx.enter_context(nc.semaphore("sem_" + e)) for e in ENGS}
    block = ctx.enter_context(nc.Block())
    kb.finalize(block, sems)


def make_in_maps(inputs):
    x = np.asarray(inputs["L0_attn_hidden_states"])
    base = {
        "win": np.ascontiguousarray(inputs["L0_attn_W_in"]),
        "convw": np.ascontiguousarray(inputs["L0_attn_conv_w"]),
        "convb": np.ascontiguousarray(inputs["L0_attn_conv_b"]),
        "dtb": np.ascontiguousarray(inputs["L0_attn_dt_bias"]),
        "alog": np.ascontiguousarray(inputs["L0_attn_A_log"]),
        "dvec": np.ascontiguousarray(inputs["L0_attn_D"]),
        "nwt": np.ascontiguousarray(inputs["L0_attn_norm_weight"]),
        "wout": np.ascontiguousarray(inputs["L0_attn_W_out"]),
        "rwt": np.ascontiguousarray(inputs["L0_ffn_router_w"]),
        "w1": np.ascontiguousarray(inputs["L0_ffn_w1"]),
        "w2": np.ascontiguousarray(inputs["L0_ffn_w2"]),
        "w3": np.ascontiguousarray(inputs["L0_ffn_w3"]),
    }
    maps = []
    for core in range(8):
        b, hh = core // 2, core % 2
        xs = np.zeros((TH, HID), np.float32)
        if hh == 0:
            xs[3:] = x[b, 0:T]
        else:
            xs[:] = x[b, T - 3: 2 * T]
        m = dict(base)
        m["xin"] = xs
        m["smask"] = np.full((128, 1), float(hh), np.float32)
        maps.append(m)
    return maps


def kernel(**inputs):
    nc, _ = build_program()
    maps = make_in_maps(inputs)
    res = run_bass_kernel_spmd(nc, maps, core_ids=list(range(8)))
    outb = np.zeros((4, 2 * T, HID), np.float32)
    for core in range(8):
        b, hh = core // 2, core % 2
        outb[b, hh * T:(hh + 1) * T] = res.results[core]["out"]
    return outb
